# version 07
# baseline (speedup 1.0000x reference)
import contextlib
import numpy as np
import ml_dtypes
import concourse.bass as bass
import concourse.mybir as mybir
from concourse.bass_utils import run_bass_kernel_spmd
from concourse.alu_op_type import AluOpType as OP

F32 = mybir.dt.float32
BF16 = mybir.dt.bfloat16
AF = mybir.ActivationFunctionType
AX = mybir.AxisListType.X

S = 2048
D = 2048
NT = S // 128
ND = D // 128
LN16 = float(np.log(16.0))
GH, HK, HV, CH = 4, 256, 512, 64
NCH = S // CH
BH = 16
LAMBDA_INIT = 0.8
RMS_EPS = 1.1920928955078125e-07
DFF = 8192
NF = DFF // 128
LN_EPS = 1e-5
GLA_EPS = 1e-5
RT128 = float(1.0 / np.sqrt(128.0))


def build_nc(upto=4, dbg=False):
    nc = bass.Bass("TRN2", target_bir_lowering=False)
    dt = nc.dram_tensor

    x_in = dt("x_in", [S, D], F32, kind="ExternalInput").ap()
    aWqt = dt("aWqt", [GH, ND, 128, 256], F32, kind="ExternalInput").ap()
    aWkt = dt("aWkt", [GH, ND, 128, 256], F32, kind="ExternalInput").ap()
    aWvt = dt("aWvt", [GH, ND, 128, 512], F32, kind="ExternalInput").ap()
    aWgt = dt("aWgt", [GH, ND, 128, 512], F32, kind="ExternalInput").ap()
    aWgk1 = dt("aWgk1", [128, 256], F32, kind="ExternalInput").ap()
    aWgk2 = dt("aWgk2", [16, 1024], F32, kind="ExternalInput").ap()
    aWo = dt("aWo", [2048, D], F32, kind="ExternalInput").ap()
    bgk2c = dt("bgk2c", [128, 8], F32, kind="ExternalInput").ap()
    bgk2r = dt("bgk2r", [1, 1024], F32, kind="ExternalInput").ap()
    gnw2 = dt("gnw2", [128, 512], F32, kind="ExternalInput").ap()
    bwqTt = dt("bwqTt", [BH, ND, 128, 128], F32, kind="ExternalInput").ap()
    bwkTt = dt("bwkTt", [BH, ND, 128, 128], F32, kind="ExternalInput").ap()
    bwvTt = dt("bwvTt", [4, ND, 128, 512], F32, kind="ExternalInput").ap()
    bwoT = dt("bwoT", [S, 2048], F32, kind="ExternalInput").ap()
    lq1r = dt("lq1r", [128, 128], F32, kind="ExternalInput").ap()
    lq2r = dt("lq2r", [128, 128], F32, kind="ExternalInput").ap()
    lk1r = dt("lk1r", [128, 128], F32, kind="ExternalInput").ap()
    lk2r = dt("lk2r", [128, 128], F32, kind="ExternalInput").ap()
    w1Tt = dt("w1Tt", [NF, D, 128], F32, kind="ExternalInput").ap()
    w2Tt = dt("w2Tt", [4, DFF, 512], F32, kind="ExternalInput").ap()
    b1c = dt("b1c", [128, NF], F32, kind="ExternalInput").ap()
    b2r = dt("b2r", [1, D], F32, kind="ExternalInput").ap()
    b1r = dt("b1r", [1, DFF], F32, kind="ExternalInput").ap()
    maskT2 = dt("maskT2", [128, 64], F32, kind="ExternalInput").ap()
    segmask = dt("segmask", [128, S], BF16, kind="ExternalInput").ap()
    ident_in = dt("ident_in", [128, 128], BF16, kind="ExternalInput").ap()
    kconst = dt("kconst", [128, 8], F32, kind="ExternalInput").ap()

    out = dt("out", [S, D], F32, kind="ExternalOutput").ap()
    if dbg:
        dbg_xn = dt("dbg_xn", [S, D], BF16, kind="ExternalOutput").ap()
        dbg_of = dt("dbg_of", [S, 2048], BF16, kind="ExternalOutput").ap()
        dbg_qg = dt("dbg_qg", [128, 4096], BF16, kind="ExternalOutput").ap()
        dbg_kg = dt("dbg_kg", [128, 4096], BF16, kind="ExternalOutput").ap()
        dbg_gc = dt("dbg_gc", [128, 4096], F32, kind="ExternalOutput").ap()
        dbg_v = dt("dbg_v", [128, 8192], BF16, kind="ExternalOutput").ap()
        dbg_S = dt("dbg_S", [128, 1024], F32, kind="ExternalOutput").ap()
        dbg_eg = dt("dbg_eg", [128, 64], F32, kind="ExternalOutput").ap()
        dbg_qr = dt("dbg_qr", [128, 4096], BF16, kind="ExternalOutput").ap()
        dbg_wt = dt("dbg_wt", [128, 1024], BF16, kind="ExternalOutput").ap()
        dbg_scl = dt("dbg_scl", [128, 8], F32, kind="ExternalOutput").ap()
        dbg_ws = dt("dbg_ws", [128, 132], F32, kind="ExternalOutput").ap()
        dbg_kr = dt("dbg_kr", [128, 4096], BF16, kind="ExternalOutput").ap()

    x_res = dt("x_res", [S, D], F32, kind="Internal").ap()
    xn_dram = dt("xn_dram", [S, D], BF16, kind="Internal").ap()
    of_dram = dt("of_dram", [S, 2048], BF16, kind="Internal").ap()
    p_dram = dt("p_dram", [S, S], BF16, kind="Internal").ap()
    w1q_dram = dt("w1q_dram", [NF, D, 128], BF16, kind="Internal").ap()
    w2q_dram = dt("w2q_dram", [4, DFF, 512], BF16, kind="Internal").ap()
    hT_dram = dt("hT_dram", [DFF, S], BF16, kind="Internal").ap()

    ctx = contextlib.ExitStack()
    EE = ctx.enter_context
    with ctx:
        bigT = EE(nc.sbuf_tensor("bigT", [128, ND * S], BF16))
        AR = EE(nc.sbuf_tensor("AR", [128, 53248], BF16))
        ARF = AR[:].bitcast(F32)
        wtile = EE(nc.sbuf_tensor("wtile", [128, 128], BF16))
        swgk1 = EE(nc.sbuf_tensor("swgk1", [128, 16 * 16], BF16))
        swgk2 = EE(nc.sbuf_tensor("swgk2", [16, 1024], BF16))
        wsums = EE(nc.sbuf_tensor("wsums", [128, 132], F32))
        scl = EE(nc.sbuf_tensor("scl", [128, 8], F32))
        onescol = EE(nc.sbuf_tensor("onescol", [128, 1], F32))
        onesrow = EE(nc.sbuf_tensor("onesrow", [1, 128], F32))
        onesbf = EE(nc.sbuf_tensor("onesbf", [1, 128], BF16))
        twocol = EE(nc.sbuf_tensor("twocol", [1, 128], BF16))
        xold = EE(nc.sbuf_tensor("xold", [128, D], F32))
        xnew = EE(nc.sbuf_tensor("xnew", [128, D], F32))
        xnsb = EE(nc.sbuf_tensor("xnsb", [128, D], BF16))
        stats = EE(nc.sbuf_tensor("stats", [128, 64], F32))
        kc = EE(nc.sbuf_tensor("kc", [128, 8], F32))
        ones512 = EE(nc.sbuf_tensor("ones512", [1, 512], BF16))
        cbgk2r = EE(nc.sbuf_tensor("cbgk2r", [1, 1024], BF16))
        twos512 = EE(nc.sbuf_tensor("twos512", [1, 512], BF16))

        cmaskT = EE(nc.sbuf_tensor("cmaskT", [128, 64], F32))
        cseg = EE(nc.sbuf_tensor("cseg", [128, S], BF16))
        cident = EE(nc.sbuf_tensor("cident", [128, 128], BF16))
        cgnw = EE(nc.sbuf_tensor("cgnw", [128, 512], F32))
        clam = EE(nc.sbuf_tensor("clam", [128, 8], F32))
        m1sb = EE(nc.sbuf_tensor("m1sb", [128, 2 * 512], F32))
        expgl = EE(nc.sbuf_tensor("expgl", [128, 2 * NCH], F32))
        bst = EE(nc.sbuf_tensor("bst", [128, 64], F32))

        PSA = EE(nc.psum_tensor("psA", [128, 2048], F32))
        PSB2 = EE(nc.psum_tensor("psB", [128, 2048], F32))
        PSUM = [PSA[:, 512 * j:512 * (j + 1)] for j in range(4)] + \
               [PSB2[:, 512 * j:512 * (j + 1)] for j in range(4)]
        PSB = [p.bitcast(BF16) for p in PSUM]

        dma_sems = ["C", "C2", "G0", "G1", "G2", "G3", "SC0", "SC1", "XO0", "XO1", "XO2", "XO3",
                    "XN", "XW0", "XW1", "XW2", "XW3", "OF0", "OF1", "H0", "H1",
                    "HT0", "HT1", "HT2", "HT3", "P0", "P1", "PT0", "PT1", "PT2", "PT3", "XT", "WQ0", "WQ1", "GC", "S"]
        SEMS = {}
        CNT = {}
        for _nm in dma_sems + ["T", "V", "A"]:
            SEMS[_nm] = EE(nc.semaphore("s" + _nm))
            CNT[_nm] = 0
        BANK = [("V", 0)] * 8
        WFREE = [0]
        blk = EE(nc.Block())
        SY, TE, VE, SC, GP = blk.sync, blk.tensor, blk.vector, blk.scalar, blk.gpsimd

        def dma(q, nm, o, i):
            q.dma_start(out=o, in_=i).then_inc(SEMS[nm], 16)
            CNT[nm] += 16
            return CNT[nm]

        def dmat(q, nm, o, i):
            q.dma_start_transpose(o, i).then_inc(SEMS[nm], 16)
            CNT[nm] += 16
            return CNT[nm]

        def inc(instr, nm, k=1):
            instr.then_inc(SEMS[nm], k)
            CNT[nm] += k
            return CNT[nm]

        def wt(q, nm, v):
            if v > 0:
                q.wait_ge(SEMS[nm], v)

        def bank_wait(t, banks):
            best = {}
            for b in banks:
                nm, v = BANK[b]
                best[nm] = max(best.get(nm, 0), v)
            for nm, v in best.items():
                wt(t, nm, v)

        def bank_used(banks, nm, v):
            for b in banks:
                BANK[b] = (nm, v)

        A_qraw, A_kraw, A_qg, A_kg = 0, 4096, 8192, 12288
        A_gkT, A_gcT, A_etmp = 16384, 16384, 24576
        A_escr = 28672
        A_vtok, A_gtok = 28672, 36864
        A_sgn = 45056
        A_zT = 47104
        A_Ssb, A_S16, A_kgtok, A_Asb = 49152, 51200, 52224, 52736
        B_M = 0
        B_QT, B_KT = 32768, 34816
        B_Vg = 36864
        B_E1, B_E2 = 45056, 47104
        B_P = B_E2
        B_PT = 49152
        F_wfp, F_wtq, F_sgn = 16384, 0, 24576
        F_hT, F_hout = 8192, 12288
        B_S2 = 16384

        def arf(c0, n):
            return ARF[:, c0 // 2: c0 // 2 + n]

        Ssb = arf(A_Ssb, 1024)
        cb1r = AR[0:1, 28672:36864]
        b2bf = AR[0:1, 26624:28672]
        lnjunk = xnsb
        S16 = AR[:, A_S16:A_S16 + 1024]
        kgtok = AR[:, A_kgtok:A_kgtok + 512]
        Asb = AR[:, A_Asb:A_Asb + 128]

        def _(sync):
            dma(sync, "C", kc[:], kconst)
            dma(sync, "C", cmaskT[:], maskT2)
            dma(sync, "C", cseg[:], segmask)
            dma(sync, "C", cident[:], ident_in)
            dma(sync, "C", cgnw[:], gnw2)
            dma(sync, "C", arf(0, 512)[:, 0:128], lq1r)
            dma(sync, "C", arf(0, 512)[:, 128:256], lq2r)
            dma(sync, "C", arf(0, 512)[:, 256:384], lk1r)
            dma(sync, "C", arf(0, 512)[:, 384:512], lk2r)
        SY(_)
        cS0 = CNT["C"]

        def _(g):
            dma(g, "C2", b2bf[0:1, :], b2r)
            dma(g, "C2", cbgk2r[0:1, :], bgk2r)
            dma(g, "C2", swgk1[:], aWgk1)
            dma(g, "C2", swgk2[:], aWgk2)
        GP(_)
        cG0 = CNT["C2"]

        def _(v):
            v.memset(onescol[:], 1.0)
            v.memset(onesrow[:], 1.0)
            v.memset(onesbf[:], 1.0)
            v.memset(twocol[:], 2.0)
            v.memset(ones512[:], 1.0)
            v.memset(twos512[:], 2.0)
            wt(v, "C", cS0)
            v.tensor_tensor(arf(0, 512)[:, 0:128], arf(0, 512)[:, 0:128], arf(0, 512)[:, 256:384], OP.mult)
            v.tensor_tensor(arf(0, 512)[:, 128:256], arf(0, 512)[:, 128:256], arf(0, 512)[:, 384:512], OP.mult)
            v.drain()
            v.reduce_sum(clam[:, 1:2], arf(0, 512)[:, 0:128], axis=AX)
            inc(v.reduce_sum(clam[:, 2:3], arf(0, 512)[:, 128:256], axis=AX), "V")
        VE(_)
        vLam = CNT["V"]

        def _(sc):
            wt(sc, "V", vLam)
            sc.activation(clam[:, 3:4], clam[:, 1:2], AF.Exp)
            sc.drain()
            inc(sc.activation(clam[:, 4:5], clam[:, 2:3], AF.Exp), "A")
        SC(_)
        aLam = CNT["A"]

        def _(v):
            wt(v, "A", aLam)
            v.tensor_tensor(clam[:, 5:6], clam[:, 3:4], clam[:, 4:5], OP.subtract)
            v.drain()
            inc(v.tensor_scalar(clam[:, 0:1], clam[:, 5:6], LAMBDA_INIT, -1.0, OP.add, OP.mult), "V")
        VE(_)

        LNDMA = {}

        def emit_ln_tile(t, src, wait_s=0, wait_nm="XO0"):
            def _(v):
                if wait_s:
                    wt(v, wait_nm, wait_s)
                v.drain()
                v.reduce_sum(stats[:, 0:1], src, axis=AX)
                v.drain()
                v.tensor_scalar(stats[:, 1:2], stats[:, 0:1], -1.0 / D, None, OP.mult)
                v.drain()
                wt(v, "XN", LNDMA.get(0, 0))
                v.scalar_tensor_tensor(xnsb[:, 0:D], src, stats[:, 1:2], src, OP.add, OP.mult,
                                       accum_out=stats[:, 2:3])
                v.drain()
                v.tensor_scalar(stats[:, 3:4], stats[:, 2:3], 1.0 / D, LN_EPS, OP.mult, OP.add)
                v.drain()
                inc(v.reciprocal(stats[:, 4:5], stats[:, 3:4]), "V")
            VE(_)
            vr = CNT["V"]

            def _(sc):
                wt(sc, "V", vr)
                inc(sc.activation(stats[:, 5:6], stats[:, 4:5], AF.Sqrt), "A")
            SC(_)
            ar = CNT["A"]

            def _(v):
                wt(v, "A", ar)
                v.tensor_tensor(stats[:, 6:7], stats[:, 5:6], stats[:, 5:6], OP.mult)
                v.drain()
                v.tensor_tensor(stats[:, 7:8], stats[:, 6:7], stats[:, 3:4], OP.mult)
                v.drain()
                v.tensor_scalar(stats[:, 8:9], stats[:, 7:8], -0.5, 1.5, OP.mult, OP.add)
                v.drain()
                v.tensor_tensor(stats[:, 9:10], stats[:, 8:9], stats[:, 5:6], OP.mult)
                v.drain()
                inc(v.tensor_scalar(xnsb[:, 0:D], src,
                                    stats[:, 1:2], stats[:, 9:10], OP.add, OP.mult), "V")
            VE(_)
            vxn = CNT["V"]

            def _(sync):
                wt(sync, "V", vxn)
                LNDMA[0] = dma(sync, "XN", xn_dram[128 * t:128 * (t + 1), :],
                                   xnsb[:, 0:D])
            SY(_)
            return CNT["XN"], vxn

        def emit_xnT(after_s, after_t=0):
            def _(sync):
                wt(sync, "XN", after_s)
                wt(sync, "T", after_t)
                for dch in range(ND):
                    dmat(sync, "XT", bigT[:, dch * S:(dch + 1) * S],
                         xn_dram[:, 128 * dch:128 * (dch + 1)])
            SY(_)
            return CNT["XT"]

        freedV = {}
        last_s = 0
        for t in range(NT):
            def _(sync):
                if t >= 1:
                    wt(sync, "V", freedV[t - 1])
                dma(sync, "XO0", xold[:, 0:D],
                    x_in[128 * t:128 * (t + 1), :])
            SY(_)
            loaded = CNT["XO0"]
            last_s, vd = emit_ln_tile(t, xold[:, 0:D], wait_s=loaded)
            freedV[t] = vd
        xnT_ok = emit_xnT(last_s)

        def _(t):
            wt(t, "C2", cG0)
            wt(t, "XT", xnT_ok)
            bank_wait(t, [0, 1, 2, 3])
            for dch in range(ND):
                for j in range(4):
                    ins = t.matmul(PSUM[j][0:16, 0:512], swgk1[:, 16 * dch:16 * dch + 16],
                                   bigT[:, dch * S + 512 * j: dch * S + 512 * (j + 1)],
                                   start=(dch == 0), stop=(dch == ND - 1))
            inc(ins, "T")
        TE(_)
        tZ = CNT["T"]

        def _(v):
            wt(v, "T", tZ)
            for j in range(4):
                ins = v.tensor_copy(AR[0:16, A_zT + 512 * j:A_zT + 512 * (j + 1)], PSUM[j][0:16, 0:512])
            inc(ins, "V")
        VE(_)
        vZ = CNT["V"]
        bank_used([0, 1, 2, 3], "V", vZ)

        chunk_end_T = tZ
        WFREE[0] = tZ
        ofLast = {0: 0, 1: 0}

        def a_proj_featmajor(h, wsrc, dst_off, wait_extra=()):
            loads = []
            for dch in range(ND):
                def _(g):
                    if dch >= 4:
                        wt(g, "T", loads[dch - 4][1])
                    else:
                        wt(g, "T", WFREE[0])
                        wt(g, "XN", CNT["XN"])
                    loads.append([dma(g, "G%d" % (dch % 4),
                                      xnsb[:, 512 * (dch % 4):512 * (dch % 4) + 256],
                                      wsrc[h, dch]), None])
                GP(_)

                def _(t):
                    wt(t, "G%d" % (dch % 4), loads[dch][0])
                    if dch == 0:
                        bank_wait(t, list(range(8)))
                        for nm, v in wait_extra:
                            wt(t, nm, v)
                    for p in range(2):
                        for j in range(4):
                            ins = t.matmul(
                                PSUM[4 * p + j][:, 0:512],
                                xnsb[:, 512 * (dch % 4) + 128 * p: 512 * (dch % 4) + 128 * (p + 1)],
                                bigT[:, dch * S + 512 * j: dch * S + 512 * (j + 1)],
                                start=(dch == 0), stop=(dch == ND - 1))
                    loads[dch][1] = inc(ins, "T")
                TE(_)
            tdone = CNT["T"]
            WFREE[0] = tdone

            def _(sc):
                wt(sc, "T", tdone)
                for p in range(2):
                    for j in range(4):
                        ins = sc.activation(AR[:, dst_off + 2048 * p + 512 * j:
                                               dst_off + 2048 * p + 512 * (j + 1)],
                                            PSUM[4 * p + j][:, 0:512], AF.Copy)
                inc(ins, "A")
            SC(_)
            bank_used(list(range(8)), "A", CNT["A"])
            return tdone, CNT["A"]

        def a_proj_tokmajor(h, wsrc, dst_off, wait_s=0):
            for half in range(2):
                loads = []
                for dch in range(ND):
                    def _(g):
                        if dch >= 4:
                            wt(g, "T", loads[dch - 4][1])
                        else:
                            wt(g, "T", WFREE[0])
                            wt(g, "XN", CNT["XN"])
                        loads.append([dma(g, "G%d" % (dch % 4),
                                          xnsb[:, 512 * (dch % 4):512 * (dch % 4 + 1)],
                                          wsrc[h, dch]), None])
                    GP(_)

                    def _(t):
                        wt(t, "G%d" % (dch % 4), loads[dch][0])
                        if dch == 0:
                            bank_wait(t, list(range(8)))
                            wt(t, "XT", wait_s)
                        for ti in range(8):
                            tt = 8 * half + ti
                            ins = t.matmul(PSUM[ti][:, 0:512],
                                           bigT[:, dch * S + 128 * tt: dch * S + 128 * (tt + 1)],
                                           xnsb[:, 512 * (dch % 4):512 * (dch % 4 + 1)],
                                           start=(dch == 0), stop=(dch == ND - 1))
                        loads[dch][1] = inc(ins, "T")
                    TE(_)
                tdone = CNT["T"]
                WFREE[0] = tdone

                def _(v):
                    wt(v, "T", tdone)
                    for ti in range(8):
                        tt = 8 * half + ti
                        ins = v.tensor_copy(AR[:, dst_off + 512 * tt: dst_off + 512 * (tt + 1)],
                                            PSUM[ti][:, 0:512])
                    inc(ins, "V")
                VE(_)
                bank_used(list(range(8)), "V", CNT["V"])
            return CNT["T"], CNT["V"]

        for h in range(GH):
            we = [("V", vZ)] if h == 0 else ()
            tq, aq = a_proj_featmajor(h, aWqt, A_qraw, wait_extra=we)
            tk, ak = a_proj_featmajor(h, aWkt, A_kraw)

            for p in range(2):
                def _(t):
                    wt(t, "V", vZ)
                    bank_wait(t, [4, 5, 6, 7])
                    for j in range(4):
                        t.matmul(PSUM[4 + j][:, 0:512],
                                 swgk2[0:16, 256 * h + 128 * p: 256 * h + 128 * (p + 1)],
                                 AR[0:16, A_zT + 512 * j:A_zT + 512 * (j + 1)],
                                 start=True, stop=False)
                    for j in range(4):
                        ins = t.matmul(PSUM[4 + j][:, 0:512],
                                       cbgk2r[0:1, 256 * h + 128 * p:256 * h + 128 * (p + 1)],
                                       ones512[0:1, :], start=False, stop=True)
                    inc(ins, "T")
                TE(_)
                tgk = CNT["T"]

                def _(sc):
                    wt(sc, "T", tgk)
                    for j in range(4):
                        sc.activation(arf(A_etmp, S)[:, 512 * j:512 * (j + 1)],
                                      PSUM[4 + j][:, 0:512], AF.Abs)
                    sc.drain()
                    for j in range(4):
                        sc.activation(arf(A_escr, S)[:, 512 * j:512 * (j + 1)],
                                      arf(A_etmp, S)[:, 512 * j:512 * (j + 1)], AF.Exp, scale=-1.0)
                    sc.drain()
                    for j in range(4):
                        ins = sc.activation(arf(A_etmp, S)[:, 512 * j:512 * (j + 1)],
                                            arf(A_escr, S)[:, 512 * j:512 * (j + 1)],
                                            AF.Ln, bias=kc[:, 5:6])
                    inc(ins, "A")
                SC(_)
                agk = CNT["A"]

                def _(v):
                    wt(v, "T", tgk)
                    for j in range(4):
                        v.tensor_scalar(arf(A_gkT + 4096 * p, S)[:, 512 * j:512 * (j + 1)],
                                        PSUM[4 + j][:, 0:512], 0.0, None, OP.min)
                    wt(v, "A", agk)
                    v.drain()
                    inc(v.tensor_tensor(arf(A_gkT + 4096 * p, S), arf(A_gkT + 4096 * p, S),
                                        arf(A_etmp, S), OP.subtract), "V")
                VE(_)
                bank_used([4, 5, 6, 7], "V", CNT["V"])

            def _(v):
                for p in range(2):
                    v.drain()
                    ins = v.tensor_tensor_scan(arf(A_gcT + 4096 * p, S), cseg[:],
                                               arf(A_gkT + 4096 * p, S), 0.0, OP.mult, OP.add)
                inc(ins, "V")
            VE(_)
            vScan = CNT["V"]

            vGates = vScan
            for p in range(2):
                def _(sc):
                    wt(sc, "V", vScan if p == 0 else CNT["V"])
                    inc(sc.activation(arf(A_etmp, S), arf(A_gcT + 4096 * p, S), AF.Exp,
                                      scale=0.0625), "A")
                SC(_)
                a1 = CNT["A"]

                def _(v):
                    wt(v, "A", a1)
                    wt(v, "A", aq)
                    inc(v.scalar_tensor_tensor(AR[:, A_qg + 2048 * p: A_qg + 2048 * (p + 1)],
                                               arf(A_etmp, S), 0.0625,
                                               AR[:, A_qraw + 2048 * p: A_qraw + 2048 * (p + 1)],
                                               OP.mult, OP.mult), "V")
                VE(_)
                v1 = CNT["V"]

                def _(sc):
                    wt(sc, "V", v1)
                    inc(sc.activation(arf(A_etmp, S), arf(A_gcT + 4096 * p, S), AF.Exp,
                                      scale=-0.0625), "A")
                SC(_)
                a2 = CNT["A"]

                def _(v):
                    wt(v, "A", a2)
                    wt(v, "A", ak)
                    inc(v.tensor_tensor(AR[:, A_kg + 2048 * p: A_kg + 2048 * (p + 1)],
                                        arf(A_etmp, S),
                                        AR[:, A_kraw + 2048 * p: A_kraw + 2048 * (p + 1)],
                                        OP.mult), "V")
                VE(_)
                v2 = CNT["V"]

                def _(sc):
                    wt(sc, "V", v2)
                    inc(sc.activation(expgl[:, NCH * p:NCH * (p + 1)],
                                      arf(A_gcT + 4096 * p, S)[:, CH - 1::CH], AF.Exp,
                                      scale=0.0625), "A")
                SC(_)
                vGates = CNT["V"]
            aGl = CNT["A"]

            tv, vv = a_proj_tokmajor(h, aWvt, A_vtok)
            tg_, vg_ = a_proj_tokmajor(h, aWgt, A_gtok)

            def _(sc):
                wt(sc, "V", vg_)
                inc(sc.activation(AR[:, 0:8192], AR[:, A_gtok:A_gtok + 8192], AF.Silu), "A")
            SC(_)
            aM1 = CNT["A"]

            def _(v):
                wt(v, "A", aM1)
                for tt_ in range(16):
                    ins = v.tensor_tensor(AR[:, 512 * tt_:512 * (tt_ + 1)],
                                          AR[:, 512 * tt_:512 * (tt_ + 1)],
                                          cgnw[:, :], OP.mult)
                inc(ins, "V")
            VE(_)

            def _(v):
                wt(v, "T", chunk_end_T)
                v.memset(Ssb, 0.0)
                inc(v.memset(S16, 0.0), "V")
            VE(_)
            vSu = {-1: CNT["V"]}

            vA_evac = {}
            vO = {}
            tO = {}
            tSd = {}
            for c in range(NCH):
                par = c % 2
                tt, rr = c // 2, (c % 2) * 64
                q0, q1 = CH * c, CH * (c + 1)

                def _(t):
                    if c == 0:
                        wt(t, "V", vGates)
                        wt(t, "A", aGl)
                        bank_wait(t, [0, 1, 2, 3, 4, 5, 6, 7])
                    if c >= 2:
                        wt(t, "V", vA_evac[c - 2])
                    for p in range(2):
                        t.matmul(PSUM[par][rr:rr + 64, 0:64],
                                 AR[:, A_kg + 2048 * p + q0: A_kg + 2048 * p + q1],
                                 AR[:, A_qg + 2048 * p + q0: A_qg + 2048 * p + q1],
                                 start=(p == 0), stop=(p == 1))
                    for p in range(2):
                        ins = t.transpose(
                            PSB[6][rr:rr + 64, 512 * par + 128 * p:512 * par + 128 * (p + 1)],
                            AR[:, A_kg + 2048 * p + q0: A_kg + 2048 * p + q1],
                            cident[:])
                    inc(ins, "T")
                TE(_)
                tA = CNT["T"]

                def _(v):
                    wt(v, "T", tA)
                    v.tensor_tensor(Asb[rr:rr + 64, 64 * par:64 * (par + 1)],
                                    PSUM[par][rr:rr + 64, 0:64], cmaskT[rr:rr + 64, :], OP.mult)
                    ins = v.tensor_copy(kgtok[rr:rr + 64, 256 * par:256 * (par + 1)],
                                        PSB[6][rr:rr + 64, 512 * par:512 * par + 256])
                    vA_evac[c] = inc(ins, "V")
                VE(_)

                def _(t):
                    wt(t, "V", vA_evac[c])
                    wt(t, "V", vSu[c - 1])
                    if c >= 2:
                        wt(t, "V", vO[c - 2])
                    t.matmul(PSUM[2 + par][rr:rr + 64, 0:512],
                             Asb[rr:rr + 64, 64 * par:64 * (par + 1)],
                             AR[rr:rr + 64, A_vtok + 512 * tt: A_vtok + 512 * (tt + 1)],
                             start=True, stop=False)
                    for p in range(2):
                        ins = t.matmul(PSUM[2 + par][rr:rr + 64, 0:512],
                                       AR[:, A_qg + 2048 * p + q0: A_qg + 2048 * p + q1],
                                       S16[:, 512 * p:512 * (p + 1)],
                                       start=False, stop=(p == 1))
                    tO[c] = inc(ins, "T")
                    for p in range(2):
                        ins = t.matmul(PSUM[4 + p][:, 0:512],
                                       kgtok[rr:rr + 64, 256 * par + 128 * p:256 * par + 128 * (p + 1)],
                                       AR[rr:rr + 64, A_vtok + 512 * tt: A_vtok + 512 * (tt + 1)],
                                       start=True, stop=True)
                    tSd[c] = inc(ins, "T")
                TE(_)

                def _(sc):
                    wt(sc, "T", tO[c])
                    ins = sc.activation(arf(A_etmp, S)[rr:rr + 64, 0:512],
                                        PSUM[2 + par][rr:rr + 64, 0:512], AF.Square,
                                        accum_out=stats[rr:rr + 64, 16 + par:17 + par])
                    inc(ins, "A")
                SC(_)
                aSil = CNT["A"]

                def _(v):
                    wt(v, "A", aSil)
                    v.tensor_scalar(stats[rr:rr + 64, 18 + par:19 + par],
                                    stats[rr:rr + 64, 16 + par:17 + par],
                                    1.0 / HV, GLA_EPS, OP.mult, OP.add)
                    v.drain()
                    inc(v.reciprocal(stats[rr:rr + 64, 20 + par:21 + par],
                                     stats[rr:rr + 64, 18 + par:19 + par]), "V")
                VE(_)
                vRms = CNT["V"]

                def _(sc):
                    wt(sc, "V", vRms)
                    inc(sc.activation(stats[rr:rr + 64, 22 + par:23 + par],
                                      stats[rr:rr + 64, 20 + par:21 + par], AF.Sqrt), "A")
                SC(_)
                aRms = CNT["A"]

                def _(v):
                    wt(v, "A", aRms)
                    wt(v, "OF%d" % par, ofLast[par])
                    v.tensor_tensor(stats[rr:rr + 64, 24 + par:25 + par],
                                    stats[rr:rr + 64, 22 + par:23 + par],
                                    stats[rr:rr + 64, 22 + par:23 + par], OP.mult)
                    v.drain()
                    v.tensor_tensor(stats[rr:rr + 64, 26 + par:27 + par],
                                    stats[rr:rr + 64, 24 + par:25 + par],
                                    stats[rr:rr + 64, 18 + par:19 + par], OP.mult)
                    v.drain()
                    v.tensor_scalar(stats[rr:rr + 64, 28 + par:29 + par],
                                    stats[rr:rr + 64, 26 + par:27 + par], -0.5, 1.5, OP.mult, OP.add)
                    v.drain()
                    v.tensor_tensor(stats[rr:rr + 64, 30 + par:31 + par],
                                    stats[rr:rr + 64, 28 + par:29 + par],
                                    stats[rr:rr + 64, 22 + par:23 + par], OP.mult)
                    v.drain()
                    ins = v.scalar_tensor_tensor(
                        AR[rr:rr + 64, A_sgn + 512 * par:A_sgn + 512 * (par + 1)],
                        PSUM[2 + par][rr:rr + 64, 0:512],
                        stats[rr:rr + 64, 30 + par:31 + par],
                        AR[rr:rr + 64, 512 * tt:512 * (tt + 1)],
                        OP.mult, OP.mult)
                    vO[c] = inc(ins, "V")
                VE(_)

                def _(sync):
                    wt(sync, "V", vO[c])
                    ofLast[par] = dma(sync, "OF%d" % par,
                                 of_dram[CH * c:CH * (c + 1), 512 * h:512 * (h + 1)],
                                 AR[rr:rr + 64, A_sgn + 512 * par:A_sgn + 512 * (par + 1)])
                SY(_)

                def _(v):
                    wt(v, "T", tSd[c])
                    for p in range(2):
                        v.tensor_tensor(Ssb[:, 512 * p:512 * (p + 1)], PSUM[4 + p][:, 0:512],
                                        Ssb[:, 512 * p:512 * (p + 1)], OP.add)
                    v.drain()
                    for p in range(2):
                        v.tensor_scalar(S16[:, 512 * p:512 * (p + 1)], Ssb[:, 512 * p:512 * (p + 1)],
                                        expgl[:, NCH * p + c:NCH * p + c + 1], None, OP.mult)
                    v.drain()
                    for p in range(2):
                        ins = v.tensor_scalar(Ssb[:, 512 * p:512 * (p + 1)],
                                              Ssb[:, 512 * p:512 * (p + 1)],
                                              expgl[:, NCH * p + c:NCH * p + c + 1], None, OP.mult)
                    vSu[c] = inc(ins, "V")
                VE(_)
            chunk_end_T = CNT["T"]
            bank_used([0, 1], "V", max(v for v in vA_evac.values()))
            bank_used([2, 3], "V", max(v for v in vO.values()))
            bank_used([4, 5], "V", vSu[NCH - 1])
            bank_used([6], "V", max(v for v in vA_evac.values()))
        last_of_S = (CNT["OF0"], CNT["OF1"])

        vGateLast = CNT["V"]
        sR_reduce = []
        sAbs = []

        def scale_drip(i):
            slot = AR[:, 2048 * (i % 2):2048 * (i % 2) + 2048]
            abslot = AR[:, 4096 + 2048 * (i % 2):4096 + 2048 * (i % 2) + 2048]
            def _(g):
                if i < 2:
                    wt(g, "T", chunk_end_T)
                    wt(g, "V", vGateLast)
                if i >= 2:
                    wt(g, "A", sAbs[i - 2])
                if i < 64:
                    dma(g, "SC%d" % (i % 2), slot, w1Tt[i].rearrange("(c p) f -> p c f", p=128))
                else:
                    j = i - 64
                    dma(g, "SC%d" % (i % 2), slot,
                        w2Tt[j // 16, 512 * (j % 16):512 * (j % 16 + 1), :]
                        .rearrange("(c p) f -> p c f", p=128))
            GP(_)
            ld = CNT["SC%d" % (i % 2)]

            def _(sc):
                wt(sc, "SC%d" % (i % 2), ld)
                if i == 0:
                    wt(sc, "V", vGateLast)
                sAbs.append(inc(sc.activation(abslot, slot, AF.Abs,
                                              accum_out=wsums[:, i:i + 1]), "A"))
            SC(_)


        def _(sync):
            wt(sync, "OF0", last_of_S[0])
            wt(sync, "OF1", last_of_S[1])
            wt(sync, "T", chunk_end_T)
            for vch in range(16):
                dmat(sync, "XT", bigT[:, vch * S:(vch + 1) * S],
                     of_dram[:, 128 * vch:128 * (vch + 1)])
        SY(_)
        ofT_ok = CNT["XT"]

        XNDMA = {0: 0}

        def sublayer_out_proj(rhs_rows_ap, lhs_tile, x_src_fn, sub, wait_s0, wait_t0,
                              post_scale=1.0, bias_mm=False, drip=None):
            freedx = {}
            lnS = 0
            for sp in range(8):
                wloads = []
                for ch in range(16):
                    gi = ch
                    if drip is not None:
                        drip(sp * 16 + ch)
                    def _(g):
                        if gi >= 4:
                            wt(g, "T", wloads[gi - 4][1])
                        else:
                            wt(g, "T", max(WFREE[0], wait_t0 if sp == 0 else 0))
                        wloads.append([dma(g, "G%d" % (gi % 4),
                                           AR[:, 36864 + 2048 * (gi % 4):
                                              36864 + 2048 * (gi % 4 + 1)],
                                           rhs_rows_ap(ch)), None])
                    GP(_)

                    def _(t):
                        wt(t, "G%d" % (gi % 4), wloads[gi][0])
                        if ch == 0:
                            if sp == 0:
                                wt(t, "XT", wait_s0)
                            bank_wait(t, list(range(8)))
                        for half in range(2):
                            tt = 2 * sp + half
                            for j in range(4):
                                ins = t.matmul(PSUM[4 * half + j][:, 0:512],
                                               lhs_tile(ch, tt),
                                               AR[:, 36864 + 2048 * (gi % 4) + 512 * j:
                                                  36864 + 2048 * (gi % 4) + 512 * (j + 1)],
                                               start=(ch == 0),
                                               stop=(ch == 15 and not bias_mm))
                        wloads[gi][1] = inc(ins, "T")
                        WFREE[0] = wloads[gi][1]
                    TE(_)
                if bias_mm:
                    def _(t):
                        for half in range(2):
                            for j in range(4):
                                ins = t.matmul(PSUM[4 * half + j][:, 0:512], twocol[0:1, :],
                                               b2bf[0:1, 512 * j:512 * (j + 1)],
                                               start=False, stop=True)
                        inc(ins, "T")
                    TE(_)
                tmm = CNT["T"]
                for half in range(2):
                    tt = 2 * sp + half
                    def _(sync):
                        if tt >= 1:
                            wt(sync, "V", freedx[tt - 1])
                        dma(sync, "XO0", xold[:, 0:D], x_src_fn(tt))
                    SY(_)
                    xl = CNT["XO0"]

                    def _(v):
                        wt(v, "T", tmm)
                        wt(v, "XO0", xl)
                        wt(v, "XW0", XNDMA[0])
                        for j in range(4):
                            ins = v.scalar_tensor_tensor(
                                xnew[:, 512 * j:512 * (j + 1)],
                                PSUM[4 * half + j][:, 0:512], post_scale,
                                xold[:, 512 * j:512 * (j + 1)],
                                OP.mult, OP.add)
                        freedx[tt] = inc(ins, "V")
                    VE(_)
                    bank_used([4 * half + j for j in range(4)], "V", freedx[tt])
                    vxn = CNT["V"]

                    def _(sync):
                        wt(sync, "V", vxn)
                        dst = out if sub == upto else x_res
                        XNDMA[0] = dma(sync, "XW0", dst[128 * tt:128 * (tt + 1), :],
                                       xnew[:, 0:D])
                    SY(_)
                    if sub < upto:
                        lnS, _vd = emit_ln_tile(tt, xnew[:, 0:D])
                        freedx[tt] = _vd
                    else:
                        lnS = CNT["S"]
            return lnS, CNT["T"]

        lnS1, tS1 = sublayer_out_proj(
            rhs_rows_ap=lambda ch: aWo[128 * ch:128 * (ch + 1), :],
            lhs_tile=lambda ch, tt: bigT[:, ch * S + 128 * tt: ch * S + 128 * (tt + 1)],
            x_src_fn=lambda tt: x_in[128 * tt:128 * (tt + 1), :],
            sub=1, wait_s0=ofT_ok, wait_t0=chunk_end_T, post_scale=1.0, bias_mm=False,
            drip=scale_drip)

        def _(v):
            wt(v, "A", sAbs[-1])
            v.reduce_sum(wsums[:, 128:129], wsums[:, 0:64], axis=AX)
            inc(v.reduce_sum(wsums[:, 129:130], wsums[:, 64:128], axis=AX), "V")
        VE(_)
        gScaleSum = CNT["V"]

        def _(t):
            wt(t, "V", gScaleSum)
            bank_wait(t, [7])
            inc(t.matmul(PSUM[7][0:1, 0:2], onescol[:], wsums[:, 128:130],
                         start=True, stop=True), "T")
        TE(_)
        tDot = CNT["T"]

        def _(v):
            wt(v, "T", tDot)
            v.tensor_scalar(scl[0:1, 2:4], PSUM[7][0:1, 0:2], 0.5 / 16777216.0, 0.5e-8,
                            OP.mult, OP.max)
            v.drain()
            inc(v.tensor_copy(scl[0:1, 6:8], scl[0:1, 2:4]), "V")
        VE(_)
        vScl = CNT["V"]

        def _(t):
            wt(t, "V", vScl)
            inc(t.matmul(PSUM[7][:, 0:2], onesrow[0:1, :], scl[0:1, 2:4],
                         start=True, stop=True), "T")
        TE(_)
        tScl = CNT["T"]

        def _(v):
            wt(v, "T", tScl)
            v.tensor_copy(scl[:, 0:1], PSUM[7][:, 0:1])
            v.tensor_copy(scl[:, 4:5], PSUM[7][:, 1:2])
            v.drain()
            v.tensor_scalar(scl[:, 1:2], scl[:, 0:1], -1.0, None, OP.mult)
            inc(v.tensor_scalar(scl[:, 5:6], scl[:, 4:5], -1.0, None, OP.mult), "V")
        VE(_)
        vSclB = CNT["V"]
        bank_used([7], "V", vSclB)


        if dbg:
            def _(sync):
                for _nm in dma_sems:
                    wt(sync, _nm, CNT[_nm])
                wt(sync, "V", CNT["V"])
                wt(sync, "A", CNT["A"])
                wt(sync, "T", CNT["T"])
                dma(sync, "S", dbg_xn, xn_dram)
                dma(sync, "S", dbg_of, of_dram)
                dma(sync, "S", dbg_qg, AR[:, A_qg:A_qg + 4096])
                dma(sync, "S", dbg_kg, AR[:, A_kg:A_kg + 4096])
                dma(sync, "S", dbg_gc, arf(A_gcT, 4096))
                dma(sync, "S", dbg_v, AR[:, A_vtok:A_vtok + 8192])
                dma(sync, "S", dbg_S, Ssb)
                dma(sync, "S", dbg_eg, expgl[:])
                dma(sync, "S", dbg_qr, AR[:, A_qraw:A_qraw + 4096])
                dma(sync, "S", dbg_wt, wtile[:])
                dma(sync, "S", dbg_scl, scl[:])
                dma(sync, "S", dbg_ws, wsums[:])
                dma(sync, "S", dbg_kr, AR[:, A_kraw:A_kraw + 4096])
            SY(_)

        def emit_ffn(sub, first, xnT_ready_s, prev_t, prev_v):
            vT = {}
            hS = {}
            wqS = {}
            def _(g):
                wt(g, "T", prev_t)
                wt(g, "V", prev_v)
                dma(g, "C2", b2bf, b2r)
                dma(g, "C2", cb1r, b1r)
            GP(_)
            b2rdy = CNT["C2"]
            for f in range(NF):
                def _(g):
                    if f < 2:
                        wt(g, "T", prev_t)
                        wt(g, "V", max(prev_v, vSclB))
                        if not first:
                            wt(g, "WQ0", CNT["WQ0"])
                            wt(g, "WQ1", CNT["WQ1"])
                    else:
                        wt(g, "T", WFREE[0] if False else phA_t[f - 2])
                    if first:
                        dma(g, "G%d" % (f % 2), arf(F_wfp + 4096 * (f % 2), 2048),
                            w1Tt[f].rearrange("(c p) f -> p c f", p=128))
                    else:
                        dma(g, "G%d" % (f % 2),
                            AR[:, F_wtq + 2048 * (f % 2):F_wtq + 2048 * (f % 2) + 2048],
                            w1q_dram[f].rearrange("(c p) f -> p c f", p=128))
                GP(_)
                gld = CNT["G%d" % (f % 2)]
                if first:
                    def _(v):
                        wt(v, "G%d" % (f % 2), gld)
                        if f >= 2:
                            wt(v, "T", phA_t[f - 2])
                            wt(v, "WQ%d" % (f % 2), wqS.get(f - 2, 0))
                            wt(v, "H%d" % (f % 2), hS[f - 2])
                        v.tensor_scalar(AR[:, F_sgn:F_sgn + 2048], arf(F_wfp + 4096 * (f % 2), 2048),
                                        scl[:, 0:1], 2.0, OP.is_ge, OP.mult)
                        v.tensor_scalar(AR[:, F_hout + 2048 * (f % 2):F_hout + 2048 * (f % 2) + 2048],
                                        arf(F_wfp + 4096 * (f % 2), 2048),
                                        scl[:, 1:2], 2.0, OP.is_le, OP.mult)
                        v.drain()
                        ins = v.tensor_tensor(AR[:, F_wtq + 2048 * (f % 2):F_wtq + 2048 * (f % 2) + 2048],
                                              AR[:, F_sgn:F_sgn + 2048],
                                              AR[:, F_hout + 2048 * (f % 2):F_hout + 2048 * (f % 2) + 2048],
                                              OP.subtract)
                        vT[f] = inc(ins, "V")
                    VE(_)

                    def _(sync):
                        wt(sync, "V", vT[f])
                        wqS[f] = dma(sync, "WQ%d" % (f % 2),
                                     w1q_dram[f].rearrange("(c p) f -> p c f", p=128),
                                     AR[:, F_wtq + 2048 * (f % 2):F_wtq + 2048 * (f % 2) + 2048])
                    SY(_)
                    wrdy = ("V", vT[f])
                else:
                    wrdy = ("G%d" % (f % 2), gld)

                def _(t):
                    wt(t, wrdy[0], wrdy[1])
                    if f == 0:
                        wt(t, "XT", xnT_ready_s)
                        bank_wait(t, list(range(8)))
                    else:
                        bank_wait(t, [4 * (f % 2) + j for j in range(4)])
                    for dch in range(ND):
                        for j in range(4):
                            ins = t.matmul(PSUM[4 * (f % 2) + j][:, 0:512],
                                           AR[:, F_wtq + 2048 * (f % 2) + 128 * dch:
                                              F_wtq + 2048 * (f % 2) + 128 * (dch + 1)],
                                           bigT[:, dch * S + 512 * j: dch * S + 512 * (j + 1)],
                                           start=(dch == 0), stop=False)
                    if f == 0:
                        wt(t, "C2", b2rdy)
                    for j in range(4):
                        ins = t.matmul(PSUM[4 * (f % 2) + j][:, 0:512],
                                       cb1r[0:1, 128 * f:128 * (f + 1)],
                                       twos512[0:1, :], start=False, stop=True)
                    phA_t[f] = inc(ins, "T")
                TE(_)

                def _(sc):
                    wt(sc, "T", phA_t[f])
                    if f >= 2:
                        wt(sc, "H%d" % (f % 2), hS[f - 2])
                    for j in range(4):
                        ins = sc.activation(AR[:, F_hout + 2048 * (f % 2) + 512 * j:
                                               F_hout + 2048 * (f % 2) + 512 * (j + 1)],
                                            PSUM[4 * (f % 2) + j][:, 0:512], AF.Gelu,
                                            scale=0.5)
                    aH = inc(ins, "A")
                    bank_used([4 * (f % 2) + j for j in range(4)], "A", aH)
                SC(_)
                aH = CNT["A"]

                def _(sync):
                    wt(sync, "A", aH)
                    hS[f] = dma(sync, "H%d" % (f % 2), hT_dram[128 * f:128 * (f + 1), :],
                                AR[:, F_hout + 2048 * (f % 2):F_hout + 2048 * (f % 2) + 2048])
                SY(_)
            phA_end_t = CNT["T"]
            phA_end_s = (CNT["H0"], CNT["H1"])

            lastdma = {}
            epiV = {}
            phB_last = phA_end_t
            for half in range(2):
                for dg in range(4):
                    fw = first and half == 0
                    hl = {}
                    vq = {}
                    wq2S = {}
                    for fch in range(64):
                        def _(sync):
                            if fch < 4 and half == 0 and dg == 0:
                                wt(sync, "H0", phA_end_s[0])
                                wt(sync, "H1", phA_end_s[1])
                            wt(sync, "T", phB_t[fch - 4] if fch >= 4 else phB_last)
                            hl[fch] = dma(sync, "HT%d" % (fch % 4),
                                          AR[:, F_hT + 1024 * (fch % 4):F_hT + 1024 * (fch % 4) + 1024],
                                          hT_dram[128 * fch:128 * (fch + 1), 1024 * half:1024 * (half + 1)])
                        SY(_)

                        wdep = 2 if fw else 4
                        def _(g):
                            wt(g, "T", phB_t[fch - wdep] if fch >= wdep else phB_last)
                            if fch < wdep and not fw:
                                wt(g, "WQ0", CNT["WQ0"])
                                wt(g, "WQ1", CNT["WQ1"])
                            if fw:
                                dma(g, "G%d" % (fch % 2), arf(F_wfp + 4096 * (fch % 2), 2048)[:, 0:512],
                                    w2Tt[dg, 128 * fch:128 * (fch + 1), :])
                            else:
                                dma(g, "G%d" % (fch % 4),
                                    AR[:, F_wtq + 512 * (fch % 4):F_wtq + 512 * (fch % 4) + 512],
                                    w2q_dram[dg, 128 * fch:128 * (fch + 1), :])
                        GP(_)
                        gld2 = CNT["G%d" % (fch % (2 if fw else 4))]
                        if fw:
                            def _(v):
                                wt(v, "G%d" % (fch % 2), gld2)
                                wt(v, "T", phB_t[fch - 2] if fch >= 2 else phB_last)
                                if fch >= 2:
                                    wt(v, "WQ%d" % (fch % 2), wq2S.get(fch - 2, 0))
                                else:
                                    wt(v, "WQ0", CNT["WQ0"])
                                    wt(v, "WQ1", CNT["WQ1"])
                                v.tensor_scalar(AR[:, F_sgn:F_sgn + 512],
                                                arf(F_wfp + 4096 * (fch % 2), 2048)[:, 0:512],
                                                scl[:, 4:5], 2.0, OP.is_ge, OP.mult)
                                v.tensor_scalar(AR[:, F_sgn + 512:F_sgn + 1024],
                                                arf(F_wfp + 4096 * (fch % 2), 2048)[:, 0:512],
                                                scl[:, 5:6], 2.0, OP.is_le, OP.mult)
                                v.drain()
                                ins = v.tensor_tensor(
                                    AR[:, F_wtq + 2048 * (fch % 2):F_wtq + 2048 * (fch % 2) + 512],
                                    AR[:, F_sgn:F_sgn + 512], AR[:, F_sgn + 512:F_sgn + 1024], OP.subtract)
                                vq[fch] = inc(ins, "V")
                            VE(_)

                            def _(sync):
                                wt(sync, "V", vq[fch])
                                wq2S[fch] = dma(sync, "WQ%d" % (fch % 2),
                                                w2q_dram[dg, 128 * fch:128 * (fch + 1), :],
                                                AR[:, F_wtq + 2048 * (fch % 2):F_wtq + 2048 * (fch % 2) + 512])
                            SY(_)
                            w2rdy = ("V", vq[fch])
                        else:
                            w2rdy = ("G%d" % (fch % 4), gld2)

                        def _(t):
                            wt(t, w2rdy[0], w2rdy[1])
                            wt(t, "HT%d" % (fch % 4), hl[fch])
                            if fch == 0:
                                bank_wait(t, list(range(8)))
                            w2sl = (F_wtq + 2048 * (fch % 2)) if fw else (F_wtq + 512 * (fch % 4))
                            for st in range(8):
                                ins = t.matmul(PSUM[st][:, 0:512],
                                               AR[:, F_hT + 1024 * (fch % 4) + 128 * st:
                                                  F_hT + 1024 * (fch % 4) + 128 * (st + 1)],
                                               AR[:, w2sl:w2sl + 512],
                                               start=(fch == 0), stop=False)
                            phB_t[fch] = inc(ins, "T")
                        TE(_)
                    def _(t):
                        wt(t, "C2", b2rdy)
                        for st in range(8):
                            ins = t.matmul(PSUM[st][:, 0:512], twocol[0:1, :],
                                           b2bf[0:1, 512 * dg:512 * (dg + 1)],
                                           start=False, stop=True)
                        inc(ins, "T")
                    TE(_)
                    tB = CNT["T"]
                    phB_last = tB
                    for st in range(8):
                        tt = 8 * half + st
                        def _(sync):
                            if half == 0 and dg == 0 and st == 0:
                                for w_ in range(4):
                                    wt(sync, "XW%d" % w_, CNT["XW%d" % w_])
                            wt(sync, "V", epiV.get(st % 4, 0))
                            dma(sync, "XO%d" % (st % 4), xold[:, 512 * (st % 4):512 * (st % 4) + 512],
                                xsrc_ffn[128 * tt:128 * (tt + 1), 512 * dg:512 * (dg + 1)])
                        SY(_)
                        xl = CNT["XO%d" % (st % 4)]

                        def _(v):
                            wt(v, "T", tB)
                            wt(v, "XO%d" % (st % 4), xl)
                            wt(v, "XW%d" % (st % 4), lastdma.get(st % 4, 0))
                            ins = v.scalar_tensor_tensor(
                                xnew[:, 512 * (st % 4):512 * (st % 4) + 512],
                                PSUM[st][:, 0:512], 0.5,
                                xold[:, 512 * (st % 4):512 * (st % 4) + 512],
                                OP.mult, OP.add)
                            epiV[st % 4] = inc(ins, "V")
                        VE(_)
                        bank_used([st], "V", epiV[st % 4])

                        def _(sync):
                            wt(sync, "V", epiV[st % 4])
                            dst = out if sub == upto else x_res
                            lastdma[st % 4] = dma(sync, "XW%d" % (st % 4),
                                                  dst[128 * tt:128 * (tt + 1), 512 * dg:512 * (dg + 1)],
                                                  xnew[:, 512 * (st % 4):512 * (st % 4) + 512])
                        SY(_)
            WFREE[0] = CNT["T"]
            return 0, CNT["T"], CNT["V"]

        phA_t = {}
        phB_t = {}
        epi_v = {}

        if upto >= 2:
            xsrc_ffn = x_res
            lnS1x = emit_xnT(lnS1, tS1)
            fS, fT, fV = emit_ffn(2, True, lnS1x, tS1, CNT["V"])
            if upto >= 3:
                freedV2 = {}
                lnS2 = 0
                for t in range(NT):
                    def _(sync):
                        if t == 0:
                            for w_ in range(4):
                                wt(sync, "XW%d" % w_, CNT["XW%d" % w_])
                        if t >= 1:
                            wt(sync, "V", freedV2[t - 1])
                        dma(sync, "XO0", xold[:, 0:D], x_res[128 * t:128 * (t + 1), :])
                    SY(_)
                    loaded = CNT["XO0"]
                    lnS2, vd2 = emit_ln_tile(t, xold[:, 0:D], wait_s=loaded)
                    freedV2[t] = vd2
                xnT_ok2 = emit_xnT(lnS2, fT)

                pvend_t = CNT["T"]
                vMw0 = [0]
                EFREE = {s: pvend_t for s in range(4)}
                for grp in range(4):
                    tvb, vvb = a_proj_tokmajor(grp, bwvTt, B_Vg, wait_s=xnT_ok2)
                    for hh in range(4):
                        h = 4 * grp + hh
                        evacs = []
                        for (wsrc, dstq, bb, scale_ap) in ((bwqTt, B_QT, 0, RT128),
                                                           (bwkTt, B_KT, 4, None)):
                            ql = []
                            for dch in range(ND):
                                def _(g):
                                    if dch >= 4:
                                        wt(g, "T", ql[dch - 4][1])
                                    else:
                                        wt(g, "T", WFREE[0])
                                        wt(g, "XN", CNT["XN"])
                                    ql.append([dma(g, "G%d" % (dch % 4),
                                                   xnsb[:, 512 * (dch % 4):512 * (dch % 4) + 128],
                                                   wsrc[h, dch]), None])
                                GP(_)

                                def _(t):
                                    wt(t, "G%d" % (dch % 4), ql[dch][0])
                                    if dch == 0:
                                        bank_wait(t, [bb + j for j in range(4)])
                                        wt(t, "XT", xnT_ok2)
                                    for j in range(4):
                                        ins = t.matmul(PSUM[bb + j][:, 0:512],
                                                       xnsb[:, 512 * (dch % 4):512 * (dch % 4) + 128],
                                                       bigT[:, dch * S + 512 * j: dch * S + 512 * (j + 1)],
                                                       start=(dch == 0), stop=(dch == ND - 1))
                                    ql[dch][1] = inc(ins, "T")
                                TE(_)
                            WFREE[0] = CNT["T"]
                            tqk = CNT["T"]

                            def _(sc):
                                wt(sc, "T", tqk)
                                for j in range(4):
                                    kw = dict(scale=scale_ap) if scale_ap is not None else {}
                                    ins = sc.activation(AR[:, dstq + 512 * j:dstq + 512 * (j + 1)],
                                                        PSUM[bb + j][:, 0:512], AF.Copy, **kw)
                                inc(ins, "A")
                            SC(_)
                            evacs.append(CNT["A"])
                            bank_used([bb + j for j in range(4)], "A", CNT["A"])
                        aQ, aK = evacs

                        for qt in range(NT):
                            def _(t):
                                if qt == 0:
                                    wt(t, "A", aK)
                                bank_wait(t, [0, 1, 2, 3])
                                for j in range(4):
                                    ins = t.matmul(PSA[:, 512 * j:512 * (j + 1)],
                                                   AR[0:64, B_QT + 128 * qt:B_QT + 128 * (qt + 1)],
                                                   AR[0:64, B_KT + 512 * j:B_KT + 512 * (j + 1)],
                                                   start=True, stop=True)
                                inc(ins, "T")
                            TE(_)
                            t1 = CNT["T"]

                            def _(sc):
                                wt(sc, "T", t1)
                                if qt == 0:
                                    wt(sc, "V", vMw0[0])
                                inc(sc.activation(AR[:, B_PT:B_PT + 2048], PSA[:, 0:2048],
                                                  AF.Exp,
                                                  accum_out=bst[:, 16 + qt:17 + qt]), "A")
                            SC(_)
                            bank_used([0, 1, 2, 3], "A", CNT["A"])

                            def _(t):
                                bank_wait(t, [4, 5, 6, 7])
                                for j in range(4):
                                    ins = t.matmul(PSB2[:, 512 * j:512 * (j + 1)],
                                                   AR[64:128, B_QT + 128 * qt:B_QT + 128 * (qt + 1)],
                                                   AR[64:128, B_KT + 512 * j:B_KT + 512 * (j + 1)],
                                                   start=True, stop=True)
                                inc(ins, "T")
                            TE(_)
                            t2 = CNT["T"]

                            def _(sc):
                                wt(sc, "T", t2)
                                inc(sc.activation(AR[:, B_PT:B_PT + 2048], PSB2[:, 0:2048],
                                                  AF.Exp,
                                                  accum_out=bst[:, 32 + qt:33 + qt]), "A")
                            SC(_)
                            bank_used([4, 5, 6, 7], "A", CNT["A"])

                        expA = {}
                        pvT = {}
                        for qb in range(4):
                            ob = 1024 * (qb % 2)
                            for kq in range(NT + 2):
                                if kq < NT:
                                    kch = kq

                                    def _(t):
                                        if kch < 2:
                                            if qb == 0:
                                                bank_wait(t, [0, 1, 2, 3])
                                            else:
                                                wt(t, "A", expA[(qb - 1, 14 + kch)])
                                        else:
                                            wt(t, "A", expA[(qb, kch - 2)])
                                        t.matmul(PSA[:, 512 * (kch % 2):512 * (kch % 2) + 512],
                                                 AR[0:64, B_KT + 128 * kch:B_KT + 128 * (kch + 1)],
                                                 AR[0:64, B_QT + 512 * qb:B_QT + 512 * (qb + 1)],
                                                 start=True, stop=True)
                                        ins = t.matmul(PSA[:, 1024 + 512 * (kch % 2):1536 + 512 * (kch % 2)],
                                                       AR[64:128, B_KT + 128 * kch:B_KT + 128 * (kch + 1)],
                                                       AR[64:128, B_QT + 512 * qb:B_QT + 512 * (qb + 1)],
                                                       start=True, stop=True)
                                        inc(ins, "T")
                                    TE(_)
                                    tS2 = CNT["T"]

                                    def _(sc):
                                        wt(sc, "T", tS2)
                                        wt(sc, "T", EFREE[kch % 4])
                                        sc.activation(AR[:, B_E1 + 512 * (kch % 4):B_E1 + 512 * (kch % 4) + 512],
                                                      PSA[:, 512 * (kch % 2):512 * (kch % 2) + 512], AF.Exp)
                                        ins = sc.activation(AR[:, B_E2 + 512 * (kch % 4):B_E2 + 512 * (kch % 4) + 512],
                                                            PSA[:, 1024 + 512 * (kch % 2):1536 + 512 * (kch % 2)],
                                                            AF.Exp)
                                        expA[(qb, kch)] = inc(ins, "A")
                                    SC(_)
                                if kq >= 2:
                                    kch = kq - 2

                                    def _(t):
                                        wt(t, "A", expA[(qb, kch)])
                                        if kch == 0:
                                            bank_wait(t, [4 + 2 * (qb % 2), 5 + 2 * (qb % 2)])
                                        for qsub in range(4):
                                            t.matmul(PSB2[:, ob + 128 * qsub:ob + 128 * (qsub + 1)],
                                                     AR[:, B_E1 + 512 * (kch % 4) + 128 * qsub:
                                                        B_E1 + 512 * (kch % 4) + 128 * (qsub + 1)],
                                                     AR[:, B_Vg + 512 * kch + 128 * hh:
                                                        B_Vg + 512 * kch + 128 * (hh + 1)],
                                                     start=(kch == 0 and qsub == 0),
                                                     stop=(kch == NT - 1 and qsub == 3))
                                        for qsub in range(4):
                                            ins = t.matmul(PSB2[:, ob + 512 + 128 * qsub:ob + 512 + 128 * (qsub + 1)],
                                                           AR[:, B_E2 + 512 * (kch % 4) + 128 * qsub:
                                                              B_E2 + 512 * (kch % 4) + 128 * (qsub + 1)],
                                                           AR[:, B_Vg + 512 * kch + 128 * hh:
                                                              B_Vg + 512 * kch + 128 * (hh + 1)],
                                                           start=(kch == 0 and qsub == 0),
                                                           stop=(kch == NT - 1 and qsub == 3))
                                        pvT[(qb, kch)] = inc(ins, "T")
                                        EFREE[kch % 4] = pvT[(qb, kch)]
                                    TE(_)

                            def _(v):
                                wt(v, "T", pvT[(qb, NT - 1)])
                                v.reciprocal(bst[:, 0:4], bst[:, 16 + 4 * qb:20 + 4 * qb])
                                v.reciprocal(bst[:, 4:8], bst[:, 32 + 4 * qb:36 + 4 * qb])
                                v.drain()
                                v.tensor_scalar(bst[:, 4:8], bst[:, 4:8], clam[:, 0:1], None,
                                                OP.mult)
                                v.drain()
                                for qsub in range(4):
                                    v.tensor_scalar(arf(B_PT + 3072, 512)[:, 128 * qsub:128 * (qsub + 1)],
                                                    PSB2[:, ob + 512 + 128 * qsub:ob + 512 + 128 * (qsub + 1)],
                                                    bst[:, 4 + qsub:5 + qsub], None, OP.mult)
                                v.drain()
                                for qsub in range(4):
                                    ins = v.scalar_tensor_tensor(
                                        arf(B_PT + 2048, 512)[:, 128 * qsub:128 * (qsub + 1)],
                                        PSB2[:, ob + 128 * qsub:ob + 128 * (qsub + 1)],
                                        bst[:, 0 + qsub:1 + qsub],
                                        arf(B_PT + 3072, 512)[:, 128 * qsub:128 * (qsub + 1)],
                                        OP.mult, OP.add)
                                inc(ins, "V")
                            VE(_)
                            vcomb = CNT["V"]

                            def _(sc):
                                wt(sc, "V", vcomb)
                                for qsub in range(4):
                                    ins = sc.activation(wtile[:, 0:128],
                                                        arf(B_PT + 2048, 512)[:, 128 * qsub:128 * (qsub + 1)],
                                                        AF.Square, accum_out=bst[:, 8 + qsub:9 + qsub])
                                inc(ins, "A")
                            SC(_)
                            aSq = CNT["A"]

                            def _(v):
                                wt(v, "A", aSq)
                                v.tensor_scalar(bst[:, 48:52], bst[:, 8:12], 1.0 / (128.0 * 0.04),
                                                RMS_EPS / 0.04, OP.mult, OP.add)
                                v.drain()
                                inc(v.reciprocal(bst[:, 52:56], bst[:, 48:52]), "V")
                            VE(_)
                            vr3 = CNT["V"]

                            def _(sc):
                                wt(sc, "V", vr3)
                                inc(sc.activation(bst[:, 56:60], bst[:, 52:56], AF.Sqrt), "A")
                            SC(_)
                            ar3 = CNT["A"]

                            def _(v):
                                wt(v, "A", ar3)
                                v.tensor_tensor(bst[:, 60:64], bst[:, 56:60], bst[:, 56:60], OP.mult)
                                v.drain()
                                v.tensor_tensor(bst[:, 60:64], bst[:, 60:64], bst[:, 48:52], OP.mult)
                                v.drain()
                                v.tensor_scalar(bst[:, 60:64], bst[:, 60:64], -0.5, 1.5, OP.mult, OP.add)
                                v.drain()
                                v.tensor_tensor(bst[:, 12:16], bst[:, 56:60], bst[:, 60:64], OP.mult)
                                v.drain()
                                for qsub in range(4):
                                    qt = 4 * qb + qsub
                                    ins = v.tensor_scalar(
                                        AR[:, B_M + 2048 * qt + h:B_M + 2048 * qt + h + 2048:16],
                                        arf(B_PT + 2048, 512)[:, 128 * qsub:128 * (qsub + 1)],
                                        bst[:, 12 + qsub:13 + qsub], None, OP.mult)
                                vMw0[0] = inc(ins, "V")
                            VE(_)
                            bank_used([4 + 2 * (qb % 2), 5 + 2 * (qb % 2)], "V", vMw0[0])
                        bank_used([0, 1, 2, 3], "A", expA[(3, NT - 1)])
                        pvend_t = pvT[(3, NT - 1)]

                lnS3, tS3 = sublayer_out_proj(
                    rhs_rows_ap=lambda ch: bwoT[128 * ch:128 * (ch + 1), :],
                    lhs_tile=lambda ch, tt: AR[:, B_M + 2048 * ch + 128 * tt:
                                               B_M + 2048 * ch + 128 * (tt + 1)],
                    x_src_fn=lambda tt: x_res[128 * tt:128 * (tt + 1), :],
                    sub=3, wait_s0=0, wait_t0=pvend_t, post_scale=1.0, bias_mm=False)

                if upto >= 4:
                    lnS3x = emit_xnT(lnS3, tS3)
                    fS4, fT4, fV4 = emit_ffn(4, False, lnS3x, tS3, CNT["V"])

        def _(sync):
            for _nm in dma_sems:
                wt(sync, _nm, CNT[_nm])
        SY(_)

    return nc


def _prep(inputs):
    f = np.ascontiguousarray
    ins = {}
    ins["aWqt"] = f(inputs["L0_attn_a_Wq"].reshape(ND, 128, GH, 256).transpose(2, 0, 1, 3))
    ins["aWkt"] = f(inputs["L0_attn_a_Wk"].reshape(ND, 128, GH, 256).transpose(2, 0, 1, 3))
    ins["aWvt"] = f(inputs["L0_attn_a_Wv"].reshape(ND, 128, GH, 512).transpose(2, 0, 1, 3))
    ins["aWgt"] = f(inputs["L0_attn_a_Wg"].reshape(ND, 128, GH, 512).transpose(2, 0, 1, 3))
    ins["aWgk1"] = f(inputs["L0_attn_a_Wgk1"].reshape(ND, 128, 16).transpose(1, 0, 2).reshape(128, 256))
    ins["aWgk2"] = f(inputs["L0_attn_a_Wgk2"])
    ins["aWo"] = f(inputs["L0_attn_a_Wo"])
    ins["bgk2c"] = f(inputs["L0_attn_a_bgk2"].reshape(8, 128).T)
    ins["bgk2r"] = f(inputs["L0_attn_a_bgk2"].reshape(1, 1024))
    ins["gnw2"] = f(np.broadcast_to(inputs["L0_attn_a_g_norm_weight"], (128, 512)))
    ins["bwqTt"] = f(inputs["L0_attn_b_wq"].T.reshape(ND, 128, 16, 128).transpose(2, 0, 1, 3))
    ins["bwkTt"] = f(inputs["L0_attn_b_wk"].T.reshape(ND, 128, 16, 128).transpose(2, 0, 1, 3))
    ins["bwvTt"] = f(inputs["L0_attn_b_wv"].T.reshape(ND, 128, 4, 512).transpose(2, 0, 1, 3))
    ins["bwoT"] = f(inputs["L0_attn_b_wo"].T)
    ins["lq1r"] = f(np.broadcast_to(inputs["L0_attn_b_lambda_q1"], (128, 128)))
    ins["lq2r"] = f(np.broadcast_to(inputs["L0_attn_b_lambda_q2"], (128, 128)))
    ins["lk1r"] = f(np.broadcast_to(inputs["L0_attn_b_lambda_k1"], (128, 128)))
    ins["lk2r"] = f(np.broadcast_to(inputs["L0_attn_b_lambda_k2"], (128, 128)))
    ins["w1Tt"] = f(inputs["L0_ffn_w1"].T.reshape(D, NF, 128).transpose(1, 0, 2))
    ins["w2Tt"] = f(inputs["L0_ffn_w2"].T.reshape(DFF, 4, 512).transpose(1, 0, 2))
    ins["b1c"] = f(inputs["L0_ffn_b1"].reshape(NF, 128).T)
    ins["b2r"] = f(inputs["L0_ffn_b2"].reshape(1, D))
    ins["b1r"] = f(inputs["L0_ffn_b1"].reshape(1, DFF))
    mt = np.tril(np.ones((64, 64), np.float32)).T
    ins["maskT2"] = f(np.concatenate([mt, mt], 0))
    sm = np.ones((128, S), np.float32)
    sm[:, 0::CH] = 0.0
    ins["segmask"] = sm.astype(ml_dtypes.bfloat16)
    ins["ident_in"] = np.eye(128).astype(ml_dtypes.bfloat16)
    kcv = np.zeros((128, 8), np.float32)
    kcv[:, 0] = -np.log(16.0)
    kcv[:, 1] = 1.0 / 16.0
    kcv[:, 2] = -1.0 / 16.0
    kcv[:, 3] = 1.0 / np.sqrt(128.0)
    kcv[:, 4] = 0.5
    kcv[:, 5] = 1.0
    kcv[:, 6] = -1.0
    ins["kconst"] = kcv
    return ins


def kernel(**inputs):
    shared = _prep(inputs)
    x = inputs["L0_attn_a_hidden_states"]
    nc = build_nc(upto=4)
    in_maps = []
    for c in range(8):
        m = dict(shared)
        m["x_in"] = np.ascontiguousarray(x[c])
        in_maps.append(m)
    res = run_bass_kernel_spmd(nc, in_maps, core_ids=list(range(8)))
    return np.stack([r["out"] for r in res.results], 0).astype(np.float32)


# version 08
# speedup vs baseline: 1.0909x; 1.0909x over previous
import contextlib
import numpy as np
import ml_dtypes
import concourse.bass as bass
import concourse.mybir as mybir
from concourse.bass_utils import run_bass_kernel_spmd
from concourse.alu_op_type import AluOpType as OP

F32 = mybir.dt.float32
BF16 = mybir.dt.bfloat16
AF = mybir.ActivationFunctionType
AX = mybir.AxisListType.X

S = 2048
D = 2048
NT = S // 128
ND = D // 128
LN16 = float(np.log(16.0))
GH, HK, HV, CH = 4, 256, 512, 64
NCH = S // CH
BH = 16
LAMBDA_INIT = 0.8
RMS_EPS = 1.1920928955078125e-07
DFF = 8192
NF = DFF // 128
LN_EPS = 1e-5
GLA_EPS = 1e-5
RT128 = float(1.0 / np.sqrt(128.0))


def build_nc(upto=4, dbg=False):
    nc = bass.Bass("TRN2", target_bir_lowering=False)
    dt = nc.dram_tensor

    x_in = dt("x_in", [S, D], F32, kind="ExternalInput").ap()
    aWqt = dt("aWqt", [GH, ND, 128, 256], F32, kind="ExternalInput").ap()
    aWkt = dt("aWkt", [GH, ND, 128, 256], F32, kind="ExternalInput").ap()
    aWvt = dt("aWvt", [GH, ND, 128, 512], F32, kind="ExternalInput").ap()
    aWgt = dt("aWgt", [GH, ND, 128, 512], F32, kind="ExternalInput").ap()
    aWgk1 = dt("aWgk1", [128, 256], F32, kind="ExternalInput").ap()
    aWgk2 = dt("aWgk2", [16, 1024], F32, kind="ExternalInput").ap()
    aWo = dt("aWo", [2048, D], F32, kind="ExternalInput").ap()
    bgk2c = dt("bgk2c", [128, 8], F32, kind="ExternalInput").ap()
    bgk2r = dt("bgk2r", [1, 1024], F32, kind="ExternalInput").ap()
    gnw2 = dt("gnw2", [128, 512], F32, kind="ExternalInput").ap()
    bwqTt = dt("bwqTt", [BH, ND, 128, 128], F32, kind="ExternalInput").ap()
    bwkTt = dt("bwkTt", [BH, ND, 128, 128], F32, kind="ExternalInput").ap()
    bwvTt = dt("bwvTt", [4, ND, 128, 512], F32, kind="ExternalInput").ap()
    bwoT = dt("bwoT", [S, 2048], F32, kind="ExternalInput").ap()
    lq1r = dt("lq1r", [128, 128], F32, kind="ExternalInput").ap()
    lq2r = dt("lq2r", [128, 128], F32, kind="ExternalInput").ap()
    lk1r = dt("lk1r", [128, 128], F32, kind="ExternalInput").ap()
    lk2r = dt("lk2r", [128, 128], F32, kind="ExternalInput").ap()
    w1Tt = dt("w1Tt", [NF, D, 128], F32, kind="ExternalInput").ap()
    w2Tt = dt("w2Tt", [4, DFF, 512], F32, kind="ExternalInput").ap()
    b1c = dt("b1c", [128, NF], F32, kind="ExternalInput").ap()
    b2r = dt("b2r", [1, D], F32, kind="ExternalInput").ap()
    b1r = dt("b1r", [1, DFF], F32, kind="ExternalInput").ap()
    maskT2 = dt("maskT2", [128, 64], F32, kind="ExternalInput").ap()
    segmask = dt("segmask", [128, S], BF16, kind="ExternalInput").ap()
    ident_in = dt("ident_in", [128, 128], BF16, kind="ExternalInput").ap()
    kconst = dt("kconst", [128, 8], F32, kind="ExternalInput").ap()

    out = dt("out", [S, D], F32, kind="ExternalOutput").ap()
    if dbg:
        dbg_xn = dt("dbg_xn", [S, D], BF16, kind="ExternalOutput").ap()
        dbg_of = dt("dbg_of", [S, 2048], BF16, kind="ExternalOutput").ap()
        dbg_qg = dt("dbg_qg", [128, 4096], BF16, kind="ExternalOutput").ap()
        dbg_kg = dt("dbg_kg", [128, 4096], BF16, kind="ExternalOutput").ap()
        dbg_gc = dt("dbg_gc", [128, 4096], F32, kind="ExternalOutput").ap()
        dbg_v = dt("dbg_v", [128, 8192], BF16, kind="ExternalOutput").ap()
        dbg_S = dt("dbg_S", [128, 1024], F32, kind="ExternalOutput").ap()
        dbg_eg = dt("dbg_eg", [128, 64], F32, kind="ExternalOutput").ap()
        dbg_qr = dt("dbg_qr", [128, 4096], BF16, kind="ExternalOutput").ap()
        dbg_wt = dt("dbg_wt", [128, 1024], BF16, kind="ExternalOutput").ap()
        dbg_scl = dt("dbg_scl", [128, 8], F32, kind="ExternalOutput").ap()
        dbg_ws = dt("dbg_ws", [128, 132], F32, kind="ExternalOutput").ap()
        dbg_kr = dt("dbg_kr", [128, 4096], BF16, kind="ExternalOutput").ap()

    x_res = dt("x_res", [S, D], F32, kind="Internal").ap()
    xn_dram = dt("xn_dram", [S, D], BF16, kind="Internal").ap()
    of_dram = dt("of_dram", [S, 2048], BF16, kind="Internal").ap()
    p_dram = dt("p_dram", [S, S], BF16, kind="Internal").ap()
    w1q_dram = dt("w1q_dram", [NF, D, 128], BF16, kind="Internal").ap()
    w2q_dram = dt("w2q_dram", [4, DFF, 512], BF16, kind="Internal").ap()
    hT_dram = dt("hT_dram", [DFF, S], BF16, kind="Internal").ap()

    ctx = contextlib.ExitStack()
    EE = ctx.enter_context
    with ctx:
        bigT = EE(nc.sbuf_tensor("bigT", [128, ND * S], BF16))
        AR = EE(nc.sbuf_tensor("AR", [128, 53248], BF16))
        ARF = AR[:].bitcast(F32)
        wtile = EE(nc.sbuf_tensor("wtile", [128, 128], BF16))
        swgk1 = EE(nc.sbuf_tensor("swgk1", [128, 16 * 16], BF16))
        swgk2 = EE(nc.sbuf_tensor("swgk2", [16, 1024], BF16))
        wsums = EE(nc.sbuf_tensor("wsums", [128, 132], F32))
        scl = EE(nc.sbuf_tensor("scl", [128, 8], F32))
        onescol = EE(nc.sbuf_tensor("onescol", [128, 1], F32))
        onesrow = EE(nc.sbuf_tensor("onesrow", [1, 128], F32))
        onesbf = EE(nc.sbuf_tensor("onesbf", [1, 128], BF16))
        twocol = EE(nc.sbuf_tensor("twocol", [1, 128], BF16))
        xold = EE(nc.sbuf_tensor("xold", [128, D], F32))
        xnew = EE(nc.sbuf_tensor("xnew", [128, D], F32))
        xnsb = EE(nc.sbuf_tensor("xnsb", [128, D], BF16))
        stats = EE(nc.sbuf_tensor("stats", [128, 64], F32))
        kc = EE(nc.sbuf_tensor("kc", [128, 8], F32))
        ones512 = EE(nc.sbuf_tensor("ones512", [1, 512], BF16))
        cbgk2r = EE(nc.sbuf_tensor("cbgk2r", [1, 1024], BF16))
        twos512 = EE(nc.sbuf_tensor("twos512", [1, 512], BF16))

        cmaskT = EE(nc.sbuf_tensor("cmaskT", [128, 64], F32))
        cseg = EE(nc.sbuf_tensor("cseg", [128, S], BF16))
        cident = EE(nc.sbuf_tensor("cident", [128, 128], BF16))
        cgnw = EE(nc.sbuf_tensor("cgnw", [128, 512], F32))
        clam = EE(nc.sbuf_tensor("clam", [128, 8], F32))
        m1sb = EE(nc.sbuf_tensor("m1sb", [128, 2 * 512], F32))
        expgl = EE(nc.sbuf_tensor("expgl", [128, 2 * NCH], F32))
        bst = EE(nc.sbuf_tensor("bst", [128, 64], F32))

        PSA = EE(nc.psum_tensor("psA", [128, 2048], F32))
        PSB2 = EE(nc.psum_tensor("psB", [128, 2048], F32))
        PSUM = [PSA[:, 512 * j:512 * (j + 1)] for j in range(4)] + \
               [PSB2[:, 512 * j:512 * (j + 1)] for j in range(4)]
        PSB = [p.bitcast(BF16) for p in PSUM]

        dma_sems = ["C", "C2", "G0", "G1", "G2", "G3", "SC0", "SC1", "XO0", "XO1", "XO2", "XO3",
                    "XN", "XW0", "XW1", "XW2", "XW3", "OF0", "OF1", "H0", "H1",
                    "HT0", "HT1", "HT2", "HT3", "P0", "P1", "PT0", "PT1", "PT2", "PT3", "XT", "WQ0", "WQ1", "GC", "S"]
        SEMS = {}
        CNT = {}
        for _nm in dma_sems + ["T", "V", "A"]:
            SEMS[_nm] = EE(nc.semaphore("s" + _nm))
            CNT[_nm] = 0
        BANK = [("V", 0)] * 8
        WFREE = [0]
        blk = EE(nc.Block())
        SY, TE, VE, SC, GP = blk.sync, blk.tensor, blk.vector, blk.scalar, blk.gpsimd

        def dma(q, nm, o, i):
            q.dma_start(out=o, in_=i).then_inc(SEMS[nm], 16)
            CNT[nm] += 16
            return CNT[nm]

        def dmat(q, nm, o, i):
            q.dma_start_transpose(o, i).then_inc(SEMS[nm], 16)
            CNT[nm] += 16
            return CNT[nm]

        def inc(instr, nm, k=1):
            instr.then_inc(SEMS[nm], k)
            CNT[nm] += k
            return CNT[nm]

        def wt(q, nm, v):
            if v > 0:
                q.wait_ge(SEMS[nm], v)

        def bank_wait(t, banks):
            best = {}
            for b in banks:
                nm, v = BANK[b]
                best[nm] = max(best.get(nm, 0), v)
            for nm, v in best.items():
                wt(t, nm, v)

        def bank_used(banks, nm, v):
            for b in banks:
                BANK[b] = (nm, v)

        A_qraw, A_kraw, A_qg, A_kg = 0, 4096, 8192, 12288
        A_gkT, A_gcT, A_etmp = 16384, 16384, 24576
        A_escr = 28672
        A_vtok, A_gtok = 28672, 36864
        A_sgn = 45056
        A_zT = 47104
        A_Ssb, A_S16, A_kgtok, A_Asb = 49152, 51200, 52224, 52736
        B_M = 0
        B_QT, B_KT = 32768, 34816
        B_Vg = 36864
        B_E1, B_E2 = 45056, 47104
        B_P = B_E2
        B_PT = 49152
        F_wfp, F_wtq, F_sgn = 16384, 0, 24576
        F_hT, F_hout = 8192, 12288
        B_S2 = 16384

        def arf(c0, n):
            return ARF[:, c0 // 2: c0 // 2 + n]

        Ssb = arf(A_Ssb, 1024)
        cb1r = AR[0:1, 28672:36864]
        b2bf = AR[0:1, 26624:28672]
        lnjunk = xnsb
        S16 = AR[:, A_S16:A_S16 + 1024]
        kgtok = AR[:, A_kgtok:A_kgtok + 512]
        Asb = AR[:, A_Asb:A_Asb + 128]

        def _(sync):
            dma(sync, "C", kc[:], kconst)
            dma(sync, "C", cmaskT[:], maskT2)
            dma(sync, "C", cseg[:], segmask)
            dma(sync, "C", cident[:], ident_in)
            dma(sync, "C", cgnw[:], gnw2)
            dma(sync, "C", arf(0, 512)[:, 0:128], lq1r)
            dma(sync, "C", arf(0, 512)[:, 128:256], lq2r)
            dma(sync, "C", arf(0, 512)[:, 256:384], lk1r)
            dma(sync, "C", arf(0, 512)[:, 384:512], lk2r)
        SY(_)
        cS0 = CNT["C"]

        def _(g):
            dma(g, "C2", b2bf[0:1, :], b2r)
            dma(g, "C2", cbgk2r[0:1, :], bgk2r)
            dma(g, "C2", swgk1[:], aWgk1)
            dma(g, "C2", swgk2[:], aWgk2)
        GP(_)
        cG0 = CNT["C2"]

        def _(v):
            v.memset(onescol[:], 1.0)
            v.memset(onesrow[:], 1.0)
            v.memset(onesbf[:], 1.0)
            v.memset(twocol[:], 2.0)
            v.memset(ones512[:], 1.0)
            v.memset(twos512[:], 2.0)
            wt(v, "C", cS0)
            v.tensor_tensor(arf(0, 512)[:, 0:128], arf(0, 512)[:, 0:128], arf(0, 512)[:, 256:384], OP.mult)
            v.tensor_tensor(arf(0, 512)[:, 128:256], arf(0, 512)[:, 128:256], arf(0, 512)[:, 384:512], OP.mult)
            v.drain()
            v.reduce_sum(clam[:, 1:2], arf(0, 512)[:, 0:128], axis=AX)
            inc(v.reduce_sum(clam[:, 2:3], arf(0, 512)[:, 128:256], axis=AX), "V")
        VE(_)
        vLam = CNT["V"]

        def _(sc):
            wt(sc, "V", vLam)
            sc.activation(clam[:, 3:4], clam[:, 1:2], AF.Exp)
            sc.drain()
            inc(sc.activation(clam[:, 4:5], clam[:, 2:3], AF.Exp), "A")
        SC(_)
        aLam = CNT["A"]

        def _(v):
            wt(v, "A", aLam)
            v.tensor_tensor(clam[:, 5:6], clam[:, 3:4], clam[:, 4:5], OP.subtract)
            v.drain()
            inc(v.tensor_scalar(clam[:, 0:1], clam[:, 5:6], LAMBDA_INIT, -1.0, OP.add, OP.mult), "V")
        VE(_)

        LNDMA = {}

        def emit_ln_tile(t, src, wait_s=0, wait_nm="XO0"):
            def _(v):
                if wait_s:
                    wt(v, wait_nm, wait_s)
                v.drain()
                v.reduce_sum(stats[:, 0:1], src, axis=AX)
                v.drain()
                v.tensor_scalar(stats[:, 1:2], stats[:, 0:1], -1.0 / D, None, OP.mult)
                v.drain()
                wt(v, "XN", LNDMA.get(0, 0))
                v.scalar_tensor_tensor(xnsb[:, 0:D], src, stats[:, 1:2], src, OP.add, OP.mult,
                                       accum_out=stats[:, 2:3])
                v.drain()
                v.tensor_scalar(stats[:, 3:4], stats[:, 2:3], 1.0 / D, LN_EPS, OP.mult, OP.add)
                v.drain()
                inc(v.reciprocal(stats[:, 4:5], stats[:, 3:4]), "V")
            VE(_)
            vr = CNT["V"]

            def _(sc):
                wt(sc, "V", vr)
                inc(sc.activation(stats[:, 5:6], stats[:, 4:5], AF.Sqrt), "A")
            SC(_)
            ar = CNT["A"]

            def _(v):
                wt(v, "A", ar)
                v.tensor_tensor(stats[:, 6:7], stats[:, 5:6], stats[:, 5:6], OP.mult)
                v.drain()
                v.tensor_tensor(stats[:, 7:8], stats[:, 6:7], stats[:, 3:4], OP.mult)
                v.drain()
                v.tensor_scalar(stats[:, 8:9], stats[:, 7:8], -0.5, 1.5, OP.mult, OP.add)
                v.drain()
                v.tensor_tensor(stats[:, 9:10], stats[:, 8:9], stats[:, 5:6], OP.mult)
                v.drain()
                inc(v.tensor_scalar(xnsb[:, 0:D], src,
                                    stats[:, 1:2], stats[:, 9:10], OP.add, OP.mult), "V")
            VE(_)
            vxn = CNT["V"]

            def _(sync):
                wt(sync, "V", vxn)
                LNDMA[0] = dma(sync, "XN", xn_dram[128 * t:128 * (t + 1), :],
                                   xnsb[:, 0:D])
            SY(_)
            return CNT["XN"], vxn

        def emit_xnT(after_s, after_t=0):
            def _(sync):
                wt(sync, "XN", after_s)
                wt(sync, "T", after_t)
                for dch in range(ND):
                    dmat(sync, "XT", bigT[:, dch * S:(dch + 1) * S],
                         xn_dram[:, 128 * dch:128 * (dch + 1)])
            SY(_)
            return CNT["XT"]

        freedV = {}
        last_s = 0
        for t in range(NT):
            def _(sync):
                if t >= 1:
                    wt(sync, "V", freedV[t - 1])
                dma(sync, "XO0", xold[:, 0:D],
                    x_in[128 * t:128 * (t + 1), :])
            SY(_)
            loaded = CNT["XO0"]
            last_s, vd = emit_ln_tile(t, xold[:, 0:D], wait_s=loaded)
            freedV[t] = vd
        xnT_ok = emit_xnT(last_s)

        def _(t):
            wt(t, "C2", cG0)
            wt(t, "XT", xnT_ok)
            bank_wait(t, [0, 1, 2, 3])
            for dch in range(ND):
                for j in range(4):
                    ins = t.matmul(PSUM[j][0:16, 0:512], swgk1[:, 16 * dch:16 * dch + 16],
                                   bigT[:, dch * S + 512 * j: dch * S + 512 * (j + 1)],
                                   start=(dch == 0), stop=(dch == ND - 1))
            inc(ins, "T")
        TE(_)
        tZ = CNT["T"]

        def _(v):
            wt(v, "T", tZ)
            for j in range(4):
                ins = v.tensor_copy(AR[0:16, A_zT + 512 * j:A_zT + 512 * (j + 1)], PSUM[j][0:16, 0:512])
            inc(ins, "V")
        VE(_)
        vZ = CNT["V"]
        bank_used([0, 1, 2, 3], "V", vZ)

        chunk_end_T = tZ
        WFREE[0] = tZ
        ofLast = {0: 0, 1: 0}

        def a_proj_featmajor(h, wsrc, dst_off, wait_extra=()):
            loads = []
            for dch in range(ND):
                def _(g):
                    if dch >= 4:
                        wt(g, "T", loads[dch - 4][1])
                    else:
                        wt(g, "T", WFREE[0])
                        wt(g, "XN", CNT["XN"])
                    loads.append([dma(g, "G%d" % (dch % 4),
                                      xnsb[:, 512 * (dch % 4):512 * (dch % 4) + 256],
                                      wsrc[h, dch]), None])
                GP(_)

                def _(t):
                    wt(t, "G%d" % (dch % 4), loads[dch][0])
                    if dch == 0:
                        bank_wait(t, list(range(8)))
                        for nm, v in wait_extra:
                            wt(t, nm, v)
                    for p in range(2):
                        for j in range(4):
                            ins = t.matmul(
                                PSUM[4 * p + j][:, 0:512],
                                xnsb[:, 512 * (dch % 4) + 128 * p: 512 * (dch % 4) + 128 * (p + 1)],
                                bigT[:, dch * S + 512 * j: dch * S + 512 * (j + 1)],
                                start=(dch == 0), stop=(dch == ND - 1))
                    loads[dch][1] = inc(ins, "T")
                TE(_)
            tdone = CNT["T"]
            WFREE[0] = tdone

            def _(sc):
                wt(sc, "T", tdone)
                for p in range(2):
                    for j in range(4):
                        ins = sc.activation(AR[:, dst_off + 2048 * p + 512 * j:
                                               dst_off + 2048 * p + 512 * (j + 1)],
                                            PSUM[4 * p + j][:, 0:512], AF.Copy)
                inc(ins, "A")
            SC(_)
            bank_used(list(range(8)), "A", CNT["A"])
            return tdone, CNT["A"]

        def a_proj_tokmajor(h, wsrc, dst_off, wait_s=0):
            for half in range(2):
                loads = []
                for dch in range(ND):
                    def _(g):
                        if dch >= 4:
                            wt(g, "T", loads[dch - 4][1])
                        else:
                            wt(g, "T", WFREE[0])
                            wt(g, "XN", CNT["XN"])
                        loads.append([dma(g, "G%d" % (dch % 4),
                                          xnsb[:, 512 * (dch % 4):512 * (dch % 4 + 1)],
                                          wsrc[h, dch]), None])
                    GP(_)

                    def _(t):
                        wt(t, "G%d" % (dch % 4), loads[dch][0])
                        if dch == 0:
                            bank_wait(t, list(range(8)))
                            wt(t, "XT", wait_s)
                        for ti in range(8):
                            tt = 8 * half + ti
                            ins = t.matmul(PSUM[ti][:, 0:512],
                                           bigT[:, dch * S + 128 * tt: dch * S + 128 * (tt + 1)],
                                           xnsb[:, 512 * (dch % 4):512 * (dch % 4 + 1)],
                                           start=(dch == 0), stop=(dch == ND - 1))
                        loads[dch][1] = inc(ins, "T")
                    TE(_)
                tdone = CNT["T"]
                WFREE[0] = tdone

                def _(v):
                    wt(v, "T", tdone)
                    for ti in range(8):
                        tt = 8 * half + ti
                        ins = v.tensor_copy(AR[:, dst_off + 512 * tt: dst_off + 512 * (tt + 1)],
                                            PSUM[ti][:, 0:512])
                    inc(ins, "V")
                VE(_)
                bank_used(list(range(8)), "V", CNT["V"])
            return CNT["T"], CNT["V"]

        for h in range(GH):
            we = [("V", vZ)] if h == 0 else ()
            tq, aq = a_proj_featmajor(h, aWqt, A_qraw, wait_extra=we)
            tk, ak = a_proj_featmajor(h, aWkt, A_kraw)

            for p in range(2):
                def _(t):
                    wt(t, "V", vZ)
                    bank_wait(t, [4, 5, 6, 7])
                    for j in range(4):
                        t.matmul(PSUM[4 + j][:, 0:512],
                                 swgk2[0:16, 256 * h + 128 * p: 256 * h + 128 * (p + 1)],
                                 AR[0:16, A_zT + 512 * j:A_zT + 512 * (j + 1)],
                                 start=True, stop=False)
                    for j in range(4):
                        ins = t.matmul(PSUM[4 + j][:, 0:512],
                                       cbgk2r[0:1, 256 * h + 128 * p:256 * h + 128 * (p + 1)],
                                       ones512[0:1, :], start=False, stop=True)
                    inc(ins, "T")
                TE(_)
                tgk = CNT["T"]

                def _(sc):
                    wt(sc, "T", tgk)
                    for j in range(4):
                        sc.activation(arf(A_etmp, S)[:, 512 * j:512 * (j + 1)],
                                      PSUM[4 + j][:, 0:512], AF.Abs)
                    sc.drain()
                    for j in range(4):
                        sc.activation(arf(A_escr, S)[:, 512 * j:512 * (j + 1)],
                                      arf(A_etmp, S)[:, 512 * j:512 * (j + 1)], AF.Exp, scale=-1.0)
                    sc.drain()
                    for j in range(4):
                        ins = sc.activation(arf(A_etmp, S)[:, 512 * j:512 * (j + 1)],
                                            arf(A_escr, S)[:, 512 * j:512 * (j + 1)],
                                            AF.Ln, bias=kc[:, 5:6])
                    inc(ins, "A")
                SC(_)
                agk = CNT["A"]

                def _(v):
                    wt(v, "T", tgk)
                    for j in range(4):
                        v.tensor_scalar(arf(A_gkT + 4096 * p, S)[:, 512 * j:512 * (j + 1)],
                                        PSUM[4 + j][:, 0:512], 0.0, None, OP.min)
                    wt(v, "A", agk)
                    v.drain()
                    inc(v.tensor_tensor(arf(A_gkT + 4096 * p, S), arf(A_gkT + 4096 * p, S),
                                        arf(A_etmp, S), OP.subtract), "V")
                VE(_)
                bank_used([4, 5, 6, 7], "V", CNT["V"])

            def _(v):
                for p in range(2):
                    v.drain()
                    ins = v.tensor_tensor_scan(arf(A_gcT + 4096 * p, S), cseg[:],
                                               arf(A_gkT + 4096 * p, S), 0.0, OP.mult, OP.add)
                inc(ins, "V")
            VE(_)
            vScan = CNT["V"]

            vGates = vScan
            for p in range(2):
                def _(sc):
                    wt(sc, "V", vScan if p == 0 else CNT["V"])
                    inc(sc.activation(arf(A_etmp, S), arf(A_gcT + 4096 * p, S), AF.Exp,
                                      scale=0.0625), "A")
                SC(_)
                a1 = CNT["A"]

                def _(v):
                    wt(v, "A", a1)
                    wt(v, "A", aq)
                    inc(v.scalar_tensor_tensor(AR[:, A_qg + 2048 * p: A_qg + 2048 * (p + 1)],
                                               arf(A_etmp, S), 0.0625,
                                               AR[:, A_qraw + 2048 * p: A_qraw + 2048 * (p + 1)],
                                               OP.mult, OP.mult), "V")
                VE(_)
                v1 = CNT["V"]

                def _(sc):
                    wt(sc, "V", v1)
                    inc(sc.activation(arf(A_etmp, S), arf(A_gcT + 4096 * p, S), AF.Exp,
                                      scale=-0.0625), "A")
                SC(_)
                a2 = CNT["A"]

                def _(v):
                    wt(v, "A", a2)
                    wt(v, "A", ak)
                    inc(v.tensor_tensor(AR[:, A_kg + 2048 * p: A_kg + 2048 * (p + 1)],
                                        arf(A_etmp, S),
                                        AR[:, A_kraw + 2048 * p: A_kraw + 2048 * (p + 1)],
                                        OP.mult), "V")
                VE(_)
                v2 = CNT["V"]

                def _(sc):
                    wt(sc, "V", v2)
                    inc(sc.activation(expgl[:, NCH * p:NCH * (p + 1)],
                                      arf(A_gcT + 4096 * p, S)[:, CH - 1::CH], AF.Exp,
                                      scale=0.0625), "A")
                SC(_)
                vGates = CNT["V"]
            aGl = CNT["A"]

            tv, vv = a_proj_tokmajor(h, aWvt, A_vtok)
            tg_, vg_ = a_proj_tokmajor(h, aWgt, A_gtok)

            def _(sc):
                wt(sc, "V", vg_)
                inc(sc.activation(AR[:, 0:8192], AR[:, A_gtok:A_gtok + 8192], AF.Silu), "A")
            SC(_)
            aM1 = CNT["A"]

            def _(v):
                wt(v, "A", aM1)
                for tt_ in range(16):
                    ins = v.tensor_tensor(AR[:, 512 * tt_:512 * (tt_ + 1)],
                                          AR[:, 512 * tt_:512 * (tt_ + 1)],
                                          cgnw[:, :], OP.mult)
                inc(ins, "V")
            VE(_)

            def _(v):
                wt(v, "T", chunk_end_T)
                v.memset(Ssb, 0.0)
                inc(v.memset(S16, 0.0), "V")
            VE(_)
            vSu = {-1: CNT["V"]}

            vA_evac = {}
            vO = {}
            tO = {}
            tSd = {}
            for c in range(NCH):
                par = c % 2
                tt, rr = c // 2, (c % 2) * 64
                q0, q1 = CH * c, CH * (c + 1)

                def _(t):
                    if c == 0:
                        wt(t, "V", vGates)
                        wt(t, "A", aGl)
                        bank_wait(t, [0, 1, 2, 3, 4, 5, 6, 7])
                    if c >= 2:
                        wt(t, "V", vA_evac[c - 2])
                    for p in range(2):
                        t.matmul(PSUM[par][rr:rr + 64, 0:64],
                                 AR[:, A_kg + 2048 * p + q0: A_kg + 2048 * p + q1],
                                 AR[:, A_qg + 2048 * p + q0: A_qg + 2048 * p + q1],
                                 start=(p == 0), stop=(p == 1))
                    for p in range(2):
                        ins = t.transpose(
                            PSB[6][rr:rr + 64, 512 * par + 128 * p:512 * par + 128 * (p + 1)],
                            AR[:, A_kg + 2048 * p + q0: A_kg + 2048 * p + q1],
                            cident[:])
                    inc(ins, "T")
                TE(_)
                tA = CNT["T"]

                def _(v):
                    wt(v, "T", tA)
                    v.tensor_tensor(Asb[rr:rr + 64, 64 * par:64 * (par + 1)],
                                    PSUM[par][rr:rr + 64, 0:64], cmaskT[rr:rr + 64, :], OP.mult)
                    ins = v.tensor_copy(kgtok[rr:rr + 64, 256 * par:256 * (par + 1)],
                                        PSB[6][rr:rr + 64, 512 * par:512 * par + 256])
                    vA_evac[c] = inc(ins, "V")
                VE(_)

                def _(t):
                    wt(t, "V", vA_evac[c])
                    wt(t, "V", vSu[c - 1])
                    if c >= 2:
                        wt(t, "V", vO[c - 2])
                    t.matmul(PSUM[2 + par][rr:rr + 64, 0:512],
                             Asb[rr:rr + 64, 64 * par:64 * (par + 1)],
                             AR[rr:rr + 64, A_vtok + 512 * tt: A_vtok + 512 * (tt + 1)],
                             start=True, stop=False)
                    for p in range(2):
                        ins = t.matmul(PSUM[2 + par][rr:rr + 64, 0:512],
                                       AR[:, A_qg + 2048 * p + q0: A_qg + 2048 * p + q1],
                                       S16[:, 512 * p:512 * (p + 1)],
                                       start=False, stop=(p == 1))
                    tO[c] = inc(ins, "T")
                    for p in range(2):
                        ins = t.matmul(PSUM[4 + p][:, 0:512],
                                       kgtok[rr:rr + 64, 256 * par + 128 * p:256 * par + 128 * (p + 1)],
                                       AR[rr:rr + 64, A_vtok + 512 * tt: A_vtok + 512 * (tt + 1)],
                                       start=True, stop=True)
                    tSd[c] = inc(ins, "T")
                TE(_)

                def _(sc):
                    wt(sc, "T", tO[c])
                    ins = sc.activation(arf(A_etmp, S)[rr:rr + 64, 0:512],
                                        PSUM[2 + par][rr:rr + 64, 0:512], AF.Square,
                                        accum_out=stats[rr:rr + 64, 16 + par:17 + par])
                    inc(ins, "A")
                SC(_)
                aSil = CNT["A"]

                def _(v):
                    wt(v, "A", aSil)
                    v.tensor_scalar(stats[rr:rr + 64, 18 + par:19 + par],
                                    stats[rr:rr + 64, 16 + par:17 + par],
                                    1.0 / HV, GLA_EPS, OP.mult, OP.add)
                    v.drain()
                    inc(v.reciprocal(stats[rr:rr + 64, 20 + par:21 + par],
                                     stats[rr:rr + 64, 18 + par:19 + par]), "V")
                VE(_)
                vRms = CNT["V"]

                def _(sc):
                    wt(sc, "V", vRms)
                    inc(sc.activation(stats[rr:rr + 64, 22 + par:23 + par],
                                      stats[rr:rr + 64, 20 + par:21 + par], AF.Sqrt), "A")
                SC(_)
                aRms = CNT["A"]

                def _(v):
                    wt(v, "A", aRms)
                    wt(v, "OF%d" % par, ofLast[par])
                    v.tensor_tensor(stats[rr:rr + 64, 24 + par:25 + par],
                                    stats[rr:rr + 64, 22 + par:23 + par],
                                    stats[rr:rr + 64, 22 + par:23 + par], OP.mult)
                    v.drain()
                    v.tensor_tensor(stats[rr:rr + 64, 26 + par:27 + par],
                                    stats[rr:rr + 64, 24 + par:25 + par],
                                    stats[rr:rr + 64, 18 + par:19 + par], OP.mult)
                    v.drain()
                    v.tensor_scalar(stats[rr:rr + 64, 28 + par:29 + par],
                                    stats[rr:rr + 64, 26 + par:27 + par], -0.5, 1.5, OP.mult, OP.add)
                    v.drain()
                    v.tensor_tensor(stats[rr:rr + 64, 30 + par:31 + par],
                                    stats[rr:rr + 64, 28 + par:29 + par],
                                    stats[rr:rr + 64, 22 + par:23 + par], OP.mult)
                    v.drain()
                    ins = v.scalar_tensor_tensor(
                        AR[rr:rr + 64, A_sgn + 512 * par:A_sgn + 512 * (par + 1)],
                        PSUM[2 + par][rr:rr + 64, 0:512],
                        stats[rr:rr + 64, 30 + par:31 + par],
                        AR[rr:rr + 64, 512 * tt:512 * (tt + 1)],
                        OP.mult, OP.mult)
                    vO[c] = inc(ins, "V")
                VE(_)

                def _(sync):
                    wt(sync, "V", vO[c])
                    ofLast[par] = dma(sync, "OF%d" % par,
                                 of_dram[CH * c:CH * (c + 1), 512 * h:512 * (h + 1)],
                                 AR[rr:rr + 64, A_sgn + 512 * par:A_sgn + 512 * (par + 1)])
                SY(_)

                def _(v):
                    wt(v, "T", tSd[c])
                    for p in range(2):
                        v.tensor_tensor(Ssb[:, 512 * p:512 * (p + 1)], PSUM[4 + p][:, 0:512],
                                        Ssb[:, 512 * p:512 * (p + 1)], OP.add)
                    v.drain()
                    for p in range(2):
                        v.tensor_scalar(S16[:, 512 * p:512 * (p + 1)], Ssb[:, 512 * p:512 * (p + 1)],
                                        expgl[:, NCH * p + c:NCH * p + c + 1], None, OP.mult)
                    v.drain()
                    for p in range(2):
                        ins = v.tensor_scalar(Ssb[:, 512 * p:512 * (p + 1)],
                                              Ssb[:, 512 * p:512 * (p + 1)],
                                              expgl[:, NCH * p + c:NCH * p + c + 1], None, OP.mult)
                    vSu[c] = inc(ins, "V")
                VE(_)
            chunk_end_T = CNT["T"]
            bank_used([0, 1], "V", max(v for v in vA_evac.values()))
            bank_used([2, 3], "V", max(v for v in vO.values()))
            bank_used([4, 5], "V", vSu[NCH - 1])
            bank_used([6], "V", max(v for v in vA_evac.values()))
        last_of_S = (CNT["OF0"], CNT["OF1"])

        vGateLast = CNT["V"]
        sR_reduce = []
        sAbs = []

        def scale_drip(i):
            slot = AR[:, 2048 * (i % 2):2048 * (i % 2) + 2048]
            abslot = AR[:, 4096 + 2048 * (i % 2):4096 + 2048 * (i % 2) + 2048]
            def _(g):
                if i < 2:
                    wt(g, "T", chunk_end_T)
                    wt(g, "V", vGateLast)
                if i >= 2:
                    wt(g, "A", sAbs[i - 2])
                if i < 64:
                    dma(g, "SC%d" % (i % 2), slot, w1Tt[i].rearrange("(c p) f -> p c f", p=128))
                else:
                    j = i - 64
                    dma(g, "SC%d" % (i % 2), slot,
                        w2Tt[j // 16, 512 * (j % 16):512 * (j % 16 + 1), :]
                        .rearrange("(c p) f -> p c f", p=128))
            GP(_)
            ld = CNT["SC%d" % (i % 2)]

            def _(sc):
                wt(sc, "SC%d" % (i % 2), ld)
                if i == 0:
                    wt(sc, "V", vGateLast)
                sAbs.append(inc(sc.activation(abslot, slot, AF.Abs,
                                              accum_out=wsums[:, i:i + 1]), "A"))
            SC(_)


        def _(sync):
            wt(sync, "OF0", last_of_S[0])
            wt(sync, "OF1", last_of_S[1])
            wt(sync, "T", chunk_end_T)
            for vch in range(16):
                dmat(sync, "XT", bigT[:, vch * S:(vch + 1) * S],
                     of_dram[:, 128 * vch:128 * (vch + 1)])
        SY(_)
        ofT_ok = CNT["XT"]

        XNDMA = {0: 0}

        def sublayer_out_proj(rhs_rows_ap, lhs_tile, x_src_fn, sub, wait_s0, wait_t0,
                              post_scale=1.0, bias_mm=False, drip=None):
            freedx = {}
            lnS = 0
            for sp in range(8):
                wloads = []
                for ch in range(16):
                    gi = ch
                    if drip is not None:
                        drip(sp * 16 + ch)
                    def _(g):
                        if gi >= 4:
                            wt(g, "T", wloads[gi - 4][1])
                        else:
                            wt(g, "T", max(WFREE[0], wait_t0 if sp == 0 else 0))
                        wloads.append([dma(g, "G%d" % (gi % 4),
                                           AR[:, 36864 + 2048 * (gi % 4):
                                              36864 + 2048 * (gi % 4 + 1)],
                                           rhs_rows_ap(ch)), None])
                    GP(_)

                    def _(t):
                        wt(t, "G%d" % (gi % 4), wloads[gi][0])
                        if ch == 0:
                            if sp == 0:
                                wt(t, "XT", wait_s0)
                            bank_wait(t, list(range(8)))
                        for half in range(2):
                            tt = 2 * sp + half
                            for j in range(4):
                                ins = t.matmul(PSUM[4 * half + j][:, 0:512],
                                               lhs_tile(ch, tt),
                                               AR[:, 36864 + 2048 * (gi % 4) + 512 * j:
                                                  36864 + 2048 * (gi % 4) + 512 * (j + 1)],
                                               start=(ch == 0),
                                               stop=(ch == 15 and not bias_mm))
                        wloads[gi][1] = inc(ins, "T")
                        WFREE[0] = wloads[gi][1]
                    TE(_)
                if bias_mm:
                    def _(t):
                        for half in range(2):
                            for j in range(4):
                                ins = t.matmul(PSUM[4 * half + j][:, 0:512], twocol[0:1, :],
                                               b2bf[0:1, 512 * j:512 * (j + 1)],
                                               start=False, stop=True)
                        inc(ins, "T")
                    TE(_)
                tmm = CNT["T"]
                for half in range(2):
                    tt = 2 * sp + half
                    def _(sync):
                        if tt >= 1:
                            wt(sync, "V", freedx[tt - 1])
                        dma(sync, "XO0", xold[:, 0:D], x_src_fn(tt))
                    SY(_)
                    xl = CNT["XO0"]

                    def _(v):
                        wt(v, "T", tmm)
                        wt(v, "XO0", xl)
                        wt(v, "XW0", XNDMA[0])
                        for j in range(4):
                            ins = v.scalar_tensor_tensor(
                                xnew[:, 512 * j:512 * (j + 1)],
                                PSUM[4 * half + j][:, 0:512], post_scale,
                                xold[:, 512 * j:512 * (j + 1)],
                                OP.mult, OP.add)
                        freedx[tt] = inc(ins, "V")
                    VE(_)
                    bank_used([4 * half + j for j in range(4)], "V", freedx[tt])
                    vxn = CNT["V"]

                    def _(sync):
                        wt(sync, "V", vxn)
                        dst = out if sub == upto else x_res
                        XNDMA[0] = dma(sync, "XW0", dst[128 * tt:128 * (tt + 1), :],
                                       xnew[:, 0:D])
                    SY(_)
                    if sub < upto:
                        lnS, _vd = emit_ln_tile(tt, xnew[:, 0:D])
                        freedx[tt] = _vd
                    else:
                        lnS = CNT["S"]
            return lnS, CNT["T"]

        lnS1, tS1 = sublayer_out_proj(
            rhs_rows_ap=lambda ch: aWo[128 * ch:128 * (ch + 1), :],
            lhs_tile=lambda ch, tt: bigT[:, ch * S + 128 * tt: ch * S + 128 * (tt + 1)],
            x_src_fn=lambda tt: x_in[128 * tt:128 * (tt + 1), :],
            sub=1, wait_s0=ofT_ok, wait_t0=chunk_end_T, post_scale=1.0, bias_mm=False,
            drip=scale_drip)

        def _(v):
            wt(v, "A", sAbs[-1])
            v.reduce_sum(wsums[:, 128:129], wsums[:, 0:64], axis=AX)
            inc(v.reduce_sum(wsums[:, 129:130], wsums[:, 64:128], axis=AX), "V")
        VE(_)
        gScaleSum = CNT["V"]

        def _(t):
            wt(t, "V", gScaleSum)
            bank_wait(t, [7])
            inc(t.matmul(PSUM[7][0:1, 0:2], onescol[:], wsums[:, 128:130],
                         start=True, stop=True), "T")
        TE(_)
        tDot = CNT["T"]

        def _(v):
            wt(v, "T", tDot)
            v.tensor_scalar(scl[0:1, 2:4], PSUM[7][0:1, 0:2], 0.5 / 16777216.0, 0.5e-8,
                            OP.mult, OP.max)
            v.drain()
            inc(v.tensor_copy(scl[0:1, 6:8], scl[0:1, 2:4]), "V")
        VE(_)
        vScl = CNT["V"]

        def _(t):
            wt(t, "V", vScl)
            inc(t.matmul(PSUM[7][:, 0:2], onesrow[0:1, :], scl[0:1, 2:4],
                         start=True, stop=True), "T")
        TE(_)
        tScl = CNT["T"]

        def _(v):
            wt(v, "T", tScl)
            v.tensor_copy(scl[:, 0:1], PSUM[7][:, 0:1])
            v.tensor_copy(scl[:, 4:5], PSUM[7][:, 1:2])
            v.drain()
            v.tensor_scalar(scl[:, 1:2], scl[:, 0:1], -1.0, None, OP.mult)
            inc(v.tensor_scalar(scl[:, 5:6], scl[:, 4:5], -1.0, None, OP.mult), "V")
        VE(_)
        vSclB = CNT["V"]
        bank_used([7], "V", vSclB)


        if dbg:
            def _(sync):
                for _nm in dma_sems:
                    wt(sync, _nm, CNT[_nm])
                wt(sync, "V", CNT["V"])
                wt(sync, "A", CNT["A"])
                wt(sync, "T", CNT["T"])
                dma(sync, "S", dbg_xn, xn_dram)
                dma(sync, "S", dbg_of, of_dram)
                dma(sync, "S", dbg_qg, AR[:, A_qg:A_qg + 4096])
                dma(sync, "S", dbg_kg, AR[:, A_kg:A_kg + 4096])
                dma(sync, "S", dbg_gc, arf(A_gcT, 4096))
                dma(sync, "S", dbg_v, AR[:, A_vtok:A_vtok + 8192])
                dma(sync, "S", dbg_S, Ssb)
                dma(sync, "S", dbg_eg, expgl[:])
                dma(sync, "S", dbg_qr, AR[:, A_qraw:A_qraw + 4096])
                dma(sync, "S", dbg_wt, wtile[:])
                dma(sync, "S", dbg_scl, scl[:])
                dma(sync, "S", dbg_ws, wsums[:])
                dma(sync, "S", dbg_kr, AR[:, A_kraw:A_kraw + 4096])
            SY(_)

        def emit_ffn(sub, first, xnT_ready_s, prev_t, prev_v):
            vT = {}
            hS = {}
            wqS = {}
            def _(g):
                wt(g, "T", prev_t)
                wt(g, "V", prev_v)
                dma(g, "C2", b2bf, b2r)
                dma(g, "C2", cb1r, b1r)
            GP(_)
            b2rdy = CNT["C2"]
            for f in range(NF):
                def _(g):
                    if f < 2:
                        wt(g, "T", prev_t)
                        wt(g, "V", max(prev_v, vSclB))
                        if not first:
                            wt(g, "WQ0", CNT["WQ0"])
                            wt(g, "WQ1", CNT["WQ1"])
                    else:
                        wt(g, "T", WFREE[0] if False else phA_t[f - 2])
                    if first:
                        dma(g, "G%d" % (f % 2), arf(F_wfp + 4096 * (f % 2), 2048),
                            w1Tt[f].rearrange("(c p) f -> p c f", p=128))
                    else:
                        dma(g, "G%d" % (f % 2),
                            AR[:, F_wtq + 2048 * (f % 2):F_wtq + 2048 * (f % 2) + 2048],
                            w1q_dram[f].rearrange("(c p) f -> p c f", p=128))
                GP(_)
                gld = CNT["G%d" % (f % 2)]
                if first:
                    def _(v):
                        wt(v, "G%d" % (f % 2), gld)
                        if f >= 2:
                            wt(v, "T", phA_t[f - 2])
                            wt(v, "WQ%d" % (f % 2), wqS.get(f - 2, 0))
                            wt(v, "H%d" % (f % 2), hS[f - 2])
                        v.tensor_scalar(AR[:, F_sgn:F_sgn + 2048], arf(F_wfp + 4096 * (f % 2), 2048),
                                        scl[:, 0:1], 2.0, OP.is_ge, OP.mult)
                        v.tensor_scalar(AR[:, F_hout + 2048 * (f % 2):F_hout + 2048 * (f % 2) + 2048],
                                        arf(F_wfp + 4096 * (f % 2), 2048),
                                        scl[:, 1:2], 2.0, OP.is_le, OP.mult)
                        v.drain()
                        ins = v.tensor_tensor(AR[:, F_wtq + 2048 * (f % 2):F_wtq + 2048 * (f % 2) + 2048],
                                              AR[:, F_sgn:F_sgn + 2048],
                                              AR[:, F_hout + 2048 * (f % 2):F_hout + 2048 * (f % 2) + 2048],
                                              OP.subtract)
                        vT[f] = inc(ins, "V")
                    VE(_)

                    def _(sync):
                        wt(sync, "V", vT[f])
                        wqS[f] = dma(sync, "WQ%d" % (f % 2),
                                     w1q_dram[f].rearrange("(c p) f -> p c f", p=128),
                                     AR[:, F_wtq + 2048 * (f % 2):F_wtq + 2048 * (f % 2) + 2048])
                    SY(_)
                    wrdy = ("V", vT[f])
                else:
                    wrdy = ("G%d" % (f % 2), gld)

                def _(t):
                    wt(t, wrdy[0], wrdy[1])
                    if f == 0:
                        wt(t, "XT", xnT_ready_s)
                        bank_wait(t, list(range(8)))
                    else:
                        bank_wait(t, [4 * (f % 2) + j for j in range(4)])
                    for dch in range(ND):
                        for j in range(4):
                            ins = t.matmul(PSUM[4 * (f % 2) + j][:, 0:512],
                                           AR[:, F_wtq + 2048 * (f % 2) + 128 * dch:
                                              F_wtq + 2048 * (f % 2) + 128 * (dch + 1)],
                                           bigT[:, dch * S + 512 * j: dch * S + 512 * (j + 1)],
                                           start=(dch == 0), stop=False)
                    if f == 0:
                        wt(t, "C2", b2rdy)
                    for j in range(4):
                        ins = t.matmul(PSUM[4 * (f % 2) + j][:, 0:512],
                                       cb1r[0:1, 128 * f:128 * (f + 1)],
                                       twos512[0:1, :], start=False, stop=True)
                    phA_t[f] = inc(ins, "T")
                TE(_)

                def _(sc):
                    wt(sc, "T", phA_t[f])
                    if f >= 2:
                        wt(sc, "H%d" % (f % 2), hS[f - 2])
                    for j in range(4):
                        ins = sc.activation(AR[:, F_hout + 2048 * (f % 2) + 512 * j:
                                               F_hout + 2048 * (f % 2) + 512 * (j + 1)],
                                            PSUM[4 * (f % 2) + j][:, 0:512], AF.Gelu,
                                            scale=0.5)
                    aH = inc(ins, "A")
                    bank_used([4 * (f % 2) + j for j in range(4)], "A", aH)
                SC(_)
                aH = CNT["A"]

                def _(sync):
                    wt(sync, "A", aH)
                    hS[f] = dma(sync, "H%d" % (f % 2), hT_dram[128 * f:128 * (f + 1), :],
                                AR[:, F_hout + 2048 * (f % 2):F_hout + 2048 * (f % 2) + 2048])
                SY(_)
            phA_end_t = CNT["T"]
            phA_end_s = (CNT["H0"], CNT["H1"])

            lastdma = {}
            epiV = {}
            phB_last = phA_end_t
            for half in range(2):
                for dg in range(4):
                    fw = first and half == 0
                    hl = {}
                    vq = {}
                    wq2S = {}
                    for fch in range(64):
                        def _(sync):
                            if fch < 4 and half == 0 and dg == 0:
                                wt(sync, "H0", phA_end_s[0])
                                wt(sync, "H1", phA_end_s[1])
                            wt(sync, "T", phB_t[fch - 4] if fch >= 4 else phB_last)
                            hl[fch] = dma(sync, "HT%d" % (fch % 4),
                                          AR[:, F_hT + 1024 * (fch % 4):F_hT + 1024 * (fch % 4) + 1024],
                                          hT_dram[128 * fch:128 * (fch + 1), 1024 * half:1024 * (half + 1)])
                        SY(_)

                        wdep = 2 if fw else 4
                        def _(g):
                            wt(g, "T", phB_t[fch - wdep] if fch >= wdep else phB_last)
                            if fch < wdep and not fw:
                                wt(g, "WQ0", CNT["WQ0"])
                                wt(g, "WQ1", CNT["WQ1"])
                            if fw:
                                dma(g, "G%d" % (fch % 2), arf(F_wfp + 4096 * (fch % 2), 2048)[:, 0:512],
                                    w2Tt[dg, 128 * fch:128 * (fch + 1), :])
                            else:
                                dma(g, "G%d" % (fch % 4),
                                    AR[:, F_wtq + 512 * (fch % 4):F_wtq + 512 * (fch % 4) + 512],
                                    w2q_dram[dg, 128 * fch:128 * (fch + 1), :])
                        GP(_)
                        gld2 = CNT["G%d" % (fch % (2 if fw else 4))]
                        if fw:
                            def _(v):
                                wt(v, "G%d" % (fch % 2), gld2)
                                wt(v, "T", phB_t[fch - 2] if fch >= 2 else phB_last)
                                if fch >= 2:
                                    wt(v, "WQ%d" % (fch % 2), wq2S.get(fch - 2, 0))
                                else:
                                    wt(v, "WQ0", CNT["WQ0"])
                                    wt(v, "WQ1", CNT["WQ1"])
                                v.tensor_scalar(AR[:, F_sgn:F_sgn + 512],
                                                arf(F_wfp + 4096 * (fch % 2), 2048)[:, 0:512],
                                                scl[:, 4:5], 2.0, OP.is_ge, OP.mult)
                                v.tensor_scalar(AR[:, F_sgn + 512:F_sgn + 1024],
                                                arf(F_wfp + 4096 * (fch % 2), 2048)[:, 0:512],
                                                scl[:, 5:6], 2.0, OP.is_le, OP.mult)
                                v.drain()
                                ins = v.tensor_tensor(
                                    AR[:, F_wtq + 2048 * (fch % 2):F_wtq + 2048 * (fch % 2) + 512],
                                    AR[:, F_sgn:F_sgn + 512], AR[:, F_sgn + 512:F_sgn + 1024], OP.subtract)
                                vq[fch] = inc(ins, "V")
                            VE(_)

                            def _(sync):
                                wt(sync, "V", vq[fch])
                                wq2S[fch] = dma(sync, "WQ%d" % (fch % 2),
                                                w2q_dram[dg, 128 * fch:128 * (fch + 1), :],
                                                AR[:, F_wtq + 2048 * (fch % 2):F_wtq + 2048 * (fch % 2) + 512])
                            SY(_)
                            w2rdy = ("V", vq[fch])
                        else:
                            w2rdy = ("G%d" % (fch % 4), gld2)

                        def _(t):
                            wt(t, w2rdy[0], w2rdy[1])
                            wt(t, "HT%d" % (fch % 4), hl[fch])
                            if fch == 0:
                                bank_wait(t, list(range(8)))
                            w2sl = (F_wtq + 2048 * (fch % 2)) if fw else (F_wtq + 512 * (fch % 4))
                            for st in range(8):
                                ins = t.matmul(PSUM[st][:, 0:512],
                                               AR[:, F_hT + 1024 * (fch % 4) + 128 * st:
                                                  F_hT + 1024 * (fch % 4) + 128 * (st + 1)],
                                               AR[:, w2sl:w2sl + 512],
                                               start=(fch == 0), stop=False)
                            phB_t[fch] = inc(ins, "T")
                        TE(_)
                    def _(t):
                        wt(t, "C2", b2rdy)
                        for st in range(8):
                            ins = t.matmul(PSUM[st][:, 0:512], twocol[0:1, :],
                                           b2bf[0:1, 512 * dg:512 * (dg + 1)],
                                           start=False, stop=True)
                        inc(ins, "T")
                    TE(_)
                    tB = CNT["T"]
                    phB_last = tB
                    xls = {}
                    for st in range(4):
                        tt = 8 * half + st
                        def _(g):
                            if half == 0 and dg == 0 and st == 0:
                                for w_ in range(4):
                                    wt(g, "XW%d" % w_, CNT["XW%d" % w_])
                                wt(g, "V", prev_v)
                            wt(g, "V", epiV.get(st % 4, 0))
                            dma(g, "XO%d" % (st % 4), xold[:, 512 * (st % 4):512 * (st % 4) + 512],
                                xsrc_ffn[128 * tt:128 * (tt + 1), 512 * dg:512 * (dg + 1)])
                        GP(_)
                        xls[st] = CNT["XO%d" % (st % 4)]
                    for st in range(8):
                        tt = 8 * half + st

                        def _(v):
                            wt(v, "T", tB)
                            wt(v, "XO%d" % (st % 4), xls[st])
                            wt(v, "XW%d" % (st % 4), lastdma.get(st % 4, 0))
                            ins = v.scalar_tensor_tensor(
                                xnew[:, 512 * (st % 4):512 * (st % 4) + 512],
                                PSUM[st][:, 0:512], 0.5,
                                xold[:, 512 * (st % 4):512 * (st % 4) + 512],
                                OP.mult, OP.add)
                            epiV[st % 4] = inc(ins, "V")
                        VE(_)
                        bank_used([st], "V", epiV[st % 4])

                        def _(sync):
                            wt(sync, "V", epiV[st % 4])
                            dst = out if sub == upto else x_res
                            lastdma[st % 4] = dma(sync, "XW%d" % (st % 4),
                                                  dst[128 * tt:128 * (tt + 1), 512 * dg:512 * (dg + 1)],
                                                  xnew[:, 512 * (st % 4):512 * (st % 4) + 512])
                        SY(_)
                        if st + 4 < 8:
                            st2 = st + 4
                            tt2 = 8 * half + st2
                            def _(g):
                                wt(g, "V", epiV[st2 % 4])
                                dma(g, "XO%d" % (st2 % 4),
                                    xold[:, 512 * (st2 % 4):512 * (st2 % 4) + 512],
                                    xsrc_ffn[128 * tt2:128 * (tt2 + 1), 512 * dg:512 * (dg + 1)])
                            GP(_)
                            xls[st2] = CNT["XO%d" % (st2 % 4)]
            WFREE[0] = CNT["T"]
            return 0, CNT["T"], CNT["V"]

        phA_t = {}
        phB_t = {}
        epi_v = {}

        if upto >= 2:
            xsrc_ffn = x_res
            lnS1x = emit_xnT(lnS1, tS1)
            fS, fT, fV = emit_ffn(2, True, lnS1x, tS1, CNT["V"])
            if upto >= 3:
                freedV2 = {}
                lnS2 = 0
                for t in range(NT):
                    def _(sync):
                        if t == 0:
                            for w_ in range(4):
                                wt(sync, "XW%d" % w_, CNT["XW%d" % w_])
                        if t >= 1:
                            wt(sync, "V", freedV2[t - 1])
                        dma(sync, "XO0", xold[:, 0:D], x_res[128 * t:128 * (t + 1), :])
                    SY(_)
                    loaded = CNT["XO0"]
                    lnS2, vd2 = emit_ln_tile(t, xold[:, 0:D], wait_s=loaded)
                    freedV2[t] = vd2
                xnT_ok2 = emit_xnT(lnS2, fT)

                pvend_t = CNT["T"]
                vMw0 = [0]
                for grp in range(4):
                    tvb, vvb = a_proj_tokmajor(grp, bwvTt, B_Vg, wait_s=xnT_ok2)
                    for hh in range(4):
                        h = 4 * grp + hh
                        evacs = []
                        for (wsrc, dstq, bb, scale_ap) in ((bwqTt, B_QT, 0, RT128),
                                                           (bwkTt, B_KT, 4, None)):
                            ql = []
                            for dch in range(ND):
                                def _(g):
                                    if dch >= 4:
                                        wt(g, "T", ql[dch - 4][1])
                                    else:
                                        wt(g, "T", WFREE[0])
                                        wt(g, "XN", CNT["XN"])
                                    ql.append([dma(g, "G%d" % (dch % 4),
                                                   xnsb[:, 512 * (dch % 4):512 * (dch % 4) + 128],
                                                   wsrc[h, dch]), None])
                                GP(_)

                                def _(t):
                                    wt(t, "G%d" % (dch % 4), ql[dch][0])
                                    if dch == 0:
                                        bank_wait(t, [bb + j for j in range(4)])
                                        wt(t, "XT", xnT_ok2)
                                    for j in range(4):
                                        ins = t.matmul(PSUM[bb + j][:, 0:512],
                                                       xnsb[:, 512 * (dch % 4):512 * (dch % 4) + 128],
                                                       bigT[:, dch * S + 512 * j: dch * S + 512 * (j + 1)],
                                                       start=(dch == 0), stop=(dch == ND - 1))
                                    ql[dch][1] = inc(ins, "T")
                                TE(_)
                            WFREE[0] = CNT["T"]
                            tqk = CNT["T"]

                            def _(sc):
                                wt(sc, "T", tqk)
                                for j in range(4):
                                    kw = dict(scale=scale_ap) if scale_ap is not None else {}
                                    ins = sc.activation(AR[:, dstq + 512 * j:dstq + 512 * (j + 1)],
                                                        PSUM[bb + j][:, 0:512], AF.Copy, **kw)
                                inc(ins, "A")
                            SC(_)
                            evacs.append(CNT["A"])
                            bank_used([bb + j for j in range(4)], "A", CNT["A"])
                        aQ, aK = evacs

                        pS = {}
                        vPd = {}
                        gPd = {}
                        for qt in range(NT):
                            def _(t):
                                if qt == 0:
                                    wt(t, "A", aK)
                                bank_wait(t, [0, 1, 2, 3])
                                for j in range(4):
                                    ins = t.matmul(PSA[:, 512 * j:512 * (j + 1)],
                                                   AR[0:64, B_QT + 128 * qt:B_QT + 128 * (qt + 1)],
                                                   AR[0:64, B_KT + 512 * j:B_KT + 512 * (j + 1)],
                                                   start=True, stop=True)
                                inc(ins, "T")
                            TE(_)
                            t1 = CNT["T"]

                            def _(v):
                                wt(v, "T", t1)
                                inc(v.reduce_max(bst[:, 0:1], PSA[:, 0:2048], axis=AX,
                                                 negate=True), "V")
                            VE(_)
                            v1 = CNT["V"]

                            def _(sc):
                                wt(sc, "V", v1)
                                if qt == 0:
                                    wt(sc, "T", pvend_t)
                                else:
                                    wt(sc, "V", gPd[qt - 1])
                                inc(sc.activation(AR[:, B_E1:B_E1 + 2048], PSA[:, 0:2048],
                                                  AF.Exp, bias=bst[:, 0:1],
                                                  accum_out=bst[:, 2:3]), "A")
                            SC(_)
                            aE1 = CNT["A"]
                            bank_used([0, 1, 2, 3], "A", aE1)

                            def _(t):
                                bank_wait(t, [4, 5, 6, 7])
                                for j in range(4):
                                    ins = t.matmul(PSB2[:, 512 * j:512 * (j + 1)],
                                                   AR[64:128, B_QT + 128 * qt:B_QT + 128 * (qt + 1)],
                                                   AR[64:128, B_KT + 512 * j:B_KT + 512 * (j + 1)],
                                                   start=True, stop=True)
                                inc(ins, "T")
                            TE(_)
                            t2 = CNT["T"]

                            def _(v):
                                wt(v, "T", t2)
                                inc(v.reduce_max(bst[:, 1:2], PSB2[:, 0:2048], axis=AX,
                                                 negate=True), "V")
                            VE(_)
                            v2 = CNT["V"]

                            def _(sc):
                                wt(sc, "V", v2)
                                if qt == 0:
                                    wt(sc, "T", pvend_t)
                                else:
                                    wt(sc, "P%d" % ((qt - 1) % 2), pS[qt - 1])
                                inc(sc.activation(AR[:, B_E2:B_E2 + 2048], PSB2[:, 0:2048],
                                                  AF.Exp, bias=bst[:, 1:2],
                                                  accum_out=bst[:, 3:4]), "A")
                            SC(_)
                            aE2 = CNT["A"]
                            bank_used([4, 5, 6, 7], "A", aE2)

                            def _(v):
                                wt(v, "A", aE2)
                                v.reciprocal(bst[:, 16 + qt:17 + qt], bst[:, 2:3])
                                v.tensor_tensor(bst[:, 4:5], clam[:, 0:1], bst[:, 2:3], OP.mult)
                                v.drain()
                                v.reciprocal(bst[:, 5:6], bst[:, 3:4])
                                v.drain()
                                v.tensor_tensor(bst[:, 6:7], bst[:, 4:5], bst[:, 5:6], OP.mult)
                                v.drain()
                                if qt >= 1:
                                    wt(v, "P%d" % ((qt - 1) % 2), pS[qt - 1])
                                ins = v.scalar_tensor_tensor(AR[:, B_P:B_P + 2048],
                                                             AR[:, B_E2:B_E2 + 2048],
                                                             bst[:, 6:7],
                                                             AR[:, B_E1:B_E1 + 2048],
                                                             OP.mult, OP.add)
                                vPd[qt] = inc(ins, "V")
                                gPd[qt] = vPd[qt]
                            VE(_)

                            def _(sync):
                                wt(sync, "V", vPd[qt])
                                pS[qt] = dma(sync, "P%d" % (qt % 2), p_dram[128 * qt:128 * (qt + 1), :],
                                             AR[:, B_P:B_P + 2048])
                            SY(_)

                        pv = {}
                        ptS = {}
                        PTSL = [B_PT, B_PT + 2048, B_E1, B_E1 + 2048]
                        for kch in range(NT):
                            def _(sync):
                                if kch == 0:
                                    wt(sync, "P0", CNT["P0"])
                                    wt(sync, "P1", CNT["P1"])
                                if kch >= 4:
                                    wt(sync, "T", pv[kch - 4])
                                ptS[kch] = dmat(sync, "PT%d" % (kch % 4),
                                                AR[:, PTSL[kch % 4]:PTSL[kch % 4] + 2048],
                                                p_dram[:, 128 * kch:128 * (kch + 1)])
                            SY(_)

                            def _(t):
                                wt(t, "PT%d" % (kch % 4), ptS[kch])
                                if kch == 0:
                                    bank_wait(t, [4, 5, 6, 7])
                                for qt in range(NT):
                                    ins = t.matmul(PSB2[:, 128 * qt:128 * (qt + 1)],
                                                   AR[:, PTSL[kch % 4] + 128 * qt:
                                                      PTSL[kch % 4] + 128 * (qt + 1)],
                                                   AR[:, B_Vg + 512 * kch + 128 * hh:
                                                      B_Vg + 512 * kch + 128 * (hh + 1)],
                                                   start=(kch == 0 and qt % 4 == 0),
                                                   stop=(kch == NT - 1 and qt % 4 == 3))
                                pv[kch] = inc(ins, "T")
                            TE(_)
                        pvend_t = CNT["T"]

                        def _(sc):
                            wt(sc, "T", pvend_t)
                            wt(sc, "V", vMw0[0])
                            for qt in range(NT):
                                ins = sc.activation(wtile[:, 0:128], PSB2[:, 128 * qt:128 * (qt + 1)],
                                                    AF.Square, accum_out=bst[:, 32 + qt:33 + qt])
                            inc(ins, "A")
                        SC(_)
                        aSq = CNT["A"]

                        def _(v):
                            wt(v, "A", aSq)
                            v.tensor_tensor(bst[:, 48:64], bst[:, 16:32], bst[:, 16:32], OP.mult)
                            v.drain()
                            v.tensor_tensor(bst[:, 32:48], bst[:, 32:48], bst[:, 48:64], OP.mult)
                            v.drain()
                            v.tensor_scalar(bst[:, 48:64], bst[:, 32:48], 1.0 / 128.0, RMS_EPS,
                                            OP.mult, OP.add)
                            v.drain()
                            inc(v.reciprocal(bst[:, 0:16], bst[:, 48:64]), "V")
                        VE(_)
                        vr3 = CNT["V"]

                        def _(sc):
                            wt(sc, "V", vr3)
                            inc(sc.activation(bst[:, 32:48], bst[:, 0:16], AF.Sqrt), "A")
                        SC(_)
                        ar3 = CNT["A"]

                        def _(v):
                            wt(v, "A", ar3)
                            v.tensor_tensor(bst[:, 0:16], bst[:, 32:48], bst[:, 32:48], OP.mult)
                            v.drain()
                            v.tensor_tensor(bst[:, 0:16], bst[:, 0:16], bst[:, 48:64], OP.mult)
                            v.drain()
                            v.tensor_scalar(bst[:, 0:16], bst[:, 0:16], -0.5, 1.5, OP.mult, OP.add)
                            v.drain()
                            v.tensor_tensor(bst[:, 32:48], bst[:, 0:16], bst[:, 32:48], OP.mult)
                            v.drain()
                            v.tensor_tensor(bst[:, 0:16], bst[:, 32:48], bst[:, 16:32], OP.mult)
                            v.drain()
                            v.tensor_scalar(bst[:, 0:16], bst[:, 0:16], 0.2, None, OP.mult)
                            v.drain()
                            for qt in range(NT):
                                ins = v.tensor_scalar(
                                    AR[:, B_M + 2048 * qt + h:B_M + 2048 * qt + h + 2048:16],
                                    PSB2[:, 128 * qt:128 * (qt + 1)],
                                    bst[:, qt:qt + 1], None, OP.mult)
                            vMw0[0] = inc(ins, "V")
                        VE(_)
                        bank_used([4, 5, 6, 7], "V", vMw0[0])

                lnS3, tS3 = sublayer_out_proj(
                    rhs_rows_ap=lambda ch: bwoT[128 * ch:128 * (ch + 1), :],
                    lhs_tile=lambda ch, tt: AR[:, B_M + 2048 * ch + 128 * tt:
                                               B_M + 2048 * ch + 128 * (tt + 1)],
                    x_src_fn=lambda tt: x_res[128 * tt:128 * (tt + 1), :],
                    sub=3, wait_s0=0, wait_t0=pvend_t, post_scale=1.0, bias_mm=False)

                if upto >= 4:
                    lnS3x = emit_xnT(lnS3, tS3)
                    fS4, fT4, fV4 = emit_ffn(4, False, lnS3x, tS3, CNT["V"])

        def _(sync):
            for _nm in dma_sems:
                wt(sync, _nm, CNT[_nm])
        SY(_)

    return nc


def _prep(inputs):
    f = np.ascontiguousarray
    ins = {}
    ins["aWqt"] = f(inputs["L0_attn_a_Wq"].reshape(ND, 128, GH, 256).transpose(2, 0, 1, 3))
    ins["aWkt"] = f(inputs["L0_attn_a_Wk"].reshape(ND, 128, GH, 256).transpose(2, 0, 1, 3))
    ins["aWvt"] = f(inputs["L0_attn_a_Wv"].reshape(ND, 128, GH, 512).transpose(2, 0, 1, 3))
    ins["aWgt"] = f(inputs["L0_attn_a_Wg"].reshape(ND, 128, GH, 512).transpose(2, 0, 1, 3))
    ins["aWgk1"] = f(inputs["L0_attn_a_Wgk1"].reshape(ND, 128, 16).transpose(1, 0, 2).reshape(128, 256))
    ins["aWgk2"] = f(inputs["L0_attn_a_Wgk2"])
    ins["aWo"] = f(inputs["L0_attn_a_Wo"])
    ins["bgk2c"] = f(inputs["L0_attn_a_bgk2"].reshape(8, 128).T)
    ins["bgk2r"] = f(inputs["L0_attn_a_bgk2"].reshape(1, 1024))
    ins["gnw2"] = f(np.broadcast_to(inputs["L0_attn_a_g_norm_weight"], (128, 512)))
    ins["bwqTt"] = f(inputs["L0_attn_b_wq"].T.reshape(ND, 128, 16, 128).transpose(2, 0, 1, 3))
    ins["bwkTt"] = f(inputs["L0_attn_b_wk"].T.reshape(ND, 128, 16, 128).transpose(2, 0, 1, 3))
    ins["bwvTt"] = f(inputs["L0_attn_b_wv"].T.reshape(ND, 128, 4, 512).transpose(2, 0, 1, 3))
    ins["bwoT"] = f(inputs["L0_attn_b_wo"].T)
    ins["lq1r"] = f(np.broadcast_to(inputs["L0_attn_b_lambda_q1"], (128, 128)))
    ins["lq2r"] = f(np.broadcast_to(inputs["L0_attn_b_lambda_q2"], (128, 128)))
    ins["lk1r"] = f(np.broadcast_to(inputs["L0_attn_b_lambda_k1"], (128, 128)))
    ins["lk2r"] = f(np.broadcast_to(inputs["L0_attn_b_lambda_k2"], (128, 128)))
    ins["w1Tt"] = f(inputs["L0_ffn_w1"].T.reshape(D, NF, 128).transpose(1, 0, 2))
    ins["w2Tt"] = f(inputs["L0_ffn_w2"].T.reshape(DFF, 4, 512).transpose(1, 0, 2))
    ins["b1c"] = f(inputs["L0_ffn_b1"].reshape(NF, 128).T)
    ins["b2r"] = f(inputs["L0_ffn_b2"].reshape(1, D))
    ins["b1r"] = f(inputs["L0_ffn_b1"].reshape(1, DFF))
    mt = np.tril(np.ones((64, 64), np.float32)).T
    ins["maskT2"] = f(np.concatenate([mt, mt], 0))
    sm = np.ones((128, S), np.float32)
    sm[:, 0::CH] = 0.0
    ins["segmask"] = sm.astype(ml_dtypes.bfloat16)
    ins["ident_in"] = np.eye(128).astype(ml_dtypes.bfloat16)
    kcv = np.zeros((128, 8), np.float32)
    kcv[:, 0] = -np.log(16.0)
    kcv[:, 1] = 1.0 / 16.0
    kcv[:, 2] = -1.0 / 16.0
    kcv[:, 3] = 1.0 / np.sqrt(128.0)
    kcv[:, 4] = 0.5
    kcv[:, 5] = 1.0
    kcv[:, 6] = -1.0
    ins["kconst"] = kcv
    return ins


def kernel(**inputs):
    shared = _prep(inputs)
    x = inputs["L0_attn_a_hidden_states"]
    nc = build_nc(upto=4)
    in_maps = []
    for c in range(8):
        m = dict(shared)
        m["x_in"] = np.ascontiguousarray(x[c])
        in_maps.append(m)
    res = run_bass_kernel_spmd(nc, in_maps, core_ids=list(range(8)))
    return np.stack([r["out"] for r in res.results], 0).astype(np.float32)
